# version 07
# baseline (speedup 1.0000x reference)
import contextlib
import itertools
import math

import numpy as np
import ml_dtypes

import concourse.bass as bass
import concourse.mybir as mybir
from concourse.bass_utils import run_bass_kernel_spmd

F32 = mybir.dt.float32
BF16 = mybir.dt.bfloat16
I32 = mybir.dt.int32
AF = mybir.ActivationFunctionType
ALU = mybir.AluOpType

S = 4096
D = 2048
NK = D // 128
H = 16
DH = 128
DHR = 64
DL = 512
NL = DL // 128
INTER = 5504
KI = INTER // 128
NT = S // 128
QG = 1024
NQG = S // QG

SCALE_MLA = 1.0 / math.sqrt(DH + DHR)
SCALE_MHA = 1.0 / math.sqrt(S)
CLAMP_HI = float(math.exp(80.0 * SCALE_MLA))
CLAMP_LO = float(math.exp(-80.0 * SCALE_MLA))
EPS = 1e-5

ENGS = ("tensor", "scalar", "vector", "sync", "gpsimd")


class Op:
    __slots__ = ("eng", "fn", "deps", "dma", "marked", "val", "gid", "epoch",
                 "chan")

    def __init__(self, eng, fn, dma):
        self.eng = eng
        self.fn = fn
        self.deps = []
        self.dma = dma
        self.marked = dma
        self.val = None
        self.gid = None
        self.epoch = 0
        self.chan = None


class Buf:
    __slots__ = ("writer", "readers", "name")

    def __init__(self, name=""):
        self.writer = None
        self.readers = []
        self.name = name


class Chan:
    __slots__ = ("key", "ops", "sem_idx", "base", "is_store")

    def __init__(self, key, is_store):
        self.key = key
        self.ops = []
        self.sem_idx = None
        self.base = 0
        self.is_store = is_store


class Plan:
    def __init__(self):
        self.ops = {e: [] for e in ENGS}
        self.gseq = 0
        self.epoch = 0
        self.chans = {}
        self.dirty_chans = set()

    def _chan(self, key, is_store=False):
        ch = self.chans.get(key)
        if ch is None:
            ch = Chan(key, is_store)
            self.chans[key] = ch
        return ch

    def add(self, eng, fn, reads=(), writes=(), dma=False, store=False,
            extra_deps=()):
        op = Op(eng, fn, dma)
        dep = set()
        for b in reads:
            if b.writer is not None:
                dep.add(b.writer)
            b.readers.append(op)
        for b in writes:
            if b.writer is not None:
                dep.add(b.writer)
            dep.update(b.readers)
            b.writer = op
            b.readers = []
        for d in extra_deps:
            if d is not None:
                dep.add(d)
        dep.discard(op)
        op.deps = list(dep)
        op.gid = self.gseq
        op.epoch = self.epoch
        self.gseq += 1
        for d in op.deps:
            assert d.gid < op.gid, "wait on future op -> deadlock"
        self.ops[eng].append(op)
        if dma:
            if store:
                ch = (self._chan(id(reads[0])) if reads
                      else self._chan(("store", eng), True))
            else:
                assert writes, "dma load must declare its target buffer"
                ch = self._chan(id(writes[0]))
            ch.ops.append(op)
            op.chan = ch
            self.dirty_chans.add(ch.key)
        return op

    def barrier(self):
        tails = []
        for e in ENGS:
            for op in reversed(self.ops[e]):
                if op.fn is not None:
                    tails.append(op)
                    break
        for key in self.dirty_chans:
            tails.append(self.chans[key].ops[-1])
        self.dirty_chans = set()
        tails = list({id(t): t for t in tails}.values())
        for e in ENGS:
            self.add(e, None, extra_deps=tails)
        self.epoch += 1

    def finalize(self, n_dma_sems):
        for e in ENGS:
            for op in self.ops[e]:
                for d in op.deps:
                    if d.dma:
                        continue
                    if d.eng == "tensor" and op.eng == "tensor":
                        continue
                    d.marked = True
        for e in ENGS:
            c = 0
            for op in self.ops[e]:
                if (not op.dma) and op.marked:
                    c += 1
                    op.val = c
        store_chs = [ch for ch in self.chans.values() if ch.is_store]
        buf_chs = [ch for ch in self.chans.values() if not ch.is_store]
        n_pool = n_dma_sems - len(store_chs)
        assert n_pool > 0
        for i, ch in enumerate(store_chs):
            ch.sem_idx = n_pool + i
            ch.base = 0
        sem_busy_until = [-1] * n_pool
        sem_count = [0] * n_pool
        for ch in sorted(buf_chs, key=lambda c: c.ops[0].gid):
            e0 = ch.ops[0].epoch
            e1 = ch.ops[-1].epoch
            placed = False
            for s in range(n_pool):
                if sem_busy_until[s] < e0:
                    ch.sem_idx = s
                    ch.base = 16 * sem_count[s]
                    sem_count[s] += len(ch.ops)
                    sem_busy_until[s] = e1
                    placed = True
                    break
            assert placed, f"out of dma sems at epoch {e0}"
        for ch in self.chans.values():
            for i, op in enumerate(ch.ops):
                op.val = ch.base + 16 * (i + 1)
        self.n_store_chs = len(store_chs)

    def emit(self, block, csems, dsems):
        plan = self

        def mk(engname):
            def emit_fn(eng):
                hw = {}
                for op in plan.ops[engname]:
                    tgt = {}
                    for d in op.deps:
                        if d.val is None:
                            continue
                        if d.dma:
                            key = ("d", d.chan.sem_idx)
                        else:
                            if d.eng == "tensor" and engname == "tensor":
                                continue
                            key = ("c", d.eng)
                        if d.val > tgt.get(key, 0):
                            tgt[key] = d.val
                    for key in sorted(tgt, key=str):
                        v = tgt[key]
                        if hw.get(key, 0) >= v:
                            continue
                        hw[key] = v
                        sem = csems[key[1]] if key[0] == "c" else dsems[key[1]]
                        eng.wait_ge(sem, v)
                    if op.fn is None:
                        continue
                    inst = op.fn(eng)
                    if op.dma:
                        inst.then_inc(dsems[op.chan.sem_idx], 16)
                    elif op.marked:
                        inst.then_inc(csems[engname], 1)
            return emit_fn

        block.tensor(mk("tensor"))
        block.scalar(mk("scalar"))
        block.vector(mk("vector"))
        block.sync(mk("sync"))
        block.gpsimd(mk("gpsimd"))


def _host_constants():
    inv_freq = 10000.0 ** (-np.arange(0, DHR, 2, dtype=np.float64) / DHR)
    pos = np.arange(S, dtype=np.float64)
    ang = pos[None, :] * inv_freq[:, None]
    cos = np.cos(ang)
    sin = np.sin(ang)
    cosq = np.empty((128, S), dtype=np.float32)
    sinq = np.empty((128, S), dtype=np.float32)
    for r in range(128):
        i = (r % DHR) // 2
        cosq[r] = cos[i]
        sinq[r] = (-sin[i]) if (r % 2 == 0) else sin[i]
    perm = np.zeros((128, 128), dtype=np.float32)
    for i in range(128):
        perm[i, i ^ 1] = 1.0
    kk = np.arange(128)[:, None]
    qq = np.arange(128)[None, :]
    um = (kk <= qq).astype(np.float32)
    ub = 1.0 - um
    consts = np.zeros((1, 8), dtype=np.float32)
    consts[0, 0] = EPS
    consts[0, 1] = SCALE_MLA
    consts[0, 2] = SCALE_MHA
    return {
        "cosq": cosq,
        "sinq": sinq,
        "perm": perm,
        "umask": um,
        "ubar": ub,
        "consts": consts,
    }


def build_nc(upto=99, dbg=None):
    nc = bass.Bass("TRN2", target_bir_lowering=False,
                   detect_race_conditions=False)
    uid = itertools.count()

    def nm(s):
        return f"{s}_{next(uid)}"

    def inp(name, shape, dt=F32):
        return nc.dram_tensor(name, list(shape), dt, kind="ExternalInput")

    x_feat = inp("x_feat", [D, S])
    wdqT = inp("wdqT", [D, DL])
    wuqT = inp("wuqT", [DL, D])
    wqrT = inp("wqrT", [DL, H * DHR])
    wdkvT = inp("wdkvT", [D, DL])
    wukT = inp("wukT", [DL, D])
    wuvT = inp("wuvT", [DL, D])
    wkrT = inp("wkrT", [D, DHR])
    woT = inp("woT", [D, D])
    wq = inp("wq", [D, D])
    wk = inp("wk", [D, D])
    wv = inp("wv", [D, D])
    wo = inp("wo", [D, D])
    gidxT = inp("gidxT", [D, INTER], I32)
    uidxT = inp("uidxT", [D, INTER], I32)
    didxT = inp("didxT", [INTER, D], I32)
    lutrow = inp("lutrow", [1, 16])
    cosq_d = inp("cosq", [128, S])
    sinq_d = inp("sinq", [128, S])
    perm_d = inp("perm", [128, 128])
    um_d = inp("umask", [128, 128])
    ub_d = inp("ubar", [128, 128])
    consts_d = inp("consts", [1, 8])

    out_feat = nc.dram_tensor("out_feat", [D, S], F32, kind="ExternalOutput")

    qc_d = nc.dram_tensor("qc_d", [D, S], BF16)
    qr_d = nc.dram_tensor("qr_d", [H * DHR, S], BF16)
    kc_d = nc.dram_tensor("kc_d", [D, S], BF16)
    v_d = nc.dram_tensor("v_d", [S, D], BF16)
    ao_d = nc.dram_tensor("ao_d", [D, S], BF16)
    r1_d = nc.dram_tensor("r1_d", [D, S], F32)
    r2_d = nc.dram_tensor("r2_d", [D, S], F32)
    r3_d = nc.dram_tensor("r3_d", [D, S], F32)
    hp_d = nc.dram_tensor("hp_d", [INTER, S], BF16)
    wg_d = nc.dram_tensor("wg_d", [D, INTER], BF16)
    wu_d = nc.dram_tensor("wu_d", [D, INTER], BF16)
    wd_d = nc.dram_tensor("wd_d", [INTER, D], BF16)
    q2_d = nc.dram_tensor("q2_d", [D, S], BF16)
    k2_d = nc.dram_tensor("k2_d", [D, S], BF16)

    p = Plan()

    es = contextlib.ExitStack()
    with es:
        def sball(stk, name, shape, dt):
            return stk.enter_context(nc.sbuf_tensor(nm(name), list(shape), dt))

        def psall(stk, name, shape, dt=F32):
            return stk.enter_context(nc.psum_tensor(nm(name), list(shape), dt))

        consts_sb = sball(es, "consts", [128, 8], F32)
        b_consts = Buf("consts")
        onescol = sball(es, "onescol", [128, 1], BF16)
        b_onescol = Buf("onescol")
        onesrow = sball(es, "onesrow", [1, 128], BF16)
        b_onesrow = Buf("onesrow")
        onestile = sball(es, "onestile", [128, 512], BF16)
        b_onestile = Buf("onestile")
        umask = sball(es, "umask", [128, 128], BF16)
        b_umask = Buf("umask")
        ubar = sball(es, "ubar", [128, 128], BF16)
        b_ubar = Buf("ubar")
        perm = sball(es, "perm", [128, 128], BF16)
        b_perm = Buf("perm")
        cosq = sball(es, "cosq", [128, S], BF16)
        b_cosq = Buf("cosq")
        sinq = sball(es, "sinq", [128, S], BF16)
        b_sinq = Buf("sinq")
        krsb = sball(es, "krsb", [DHR, S], BF16)
        b_krsb = Buf("krsb")
        lutb = sball(es, "lutb", [128, 16], F32)
        b_lutb = Buf("lutb")

        C_EPS = consts_sb[0:1, 0:1]
        C_SMLA = consts_sb[:, 1:2]
        C_SMHA = consts_sb[:, 2:3]

        p.add("sync", lambda e: e.dma_start(
            consts_sb[:, :], bass.AP(consts_d, 0, [[0, 128], [1, 8]])),
            writes=[b_consts], dma=True)
        p.add("sync", lambda e: e.dma_start(
            lutb[:, :], bass.AP(lutrow, 0, [[0, 128], [1, 16]])),
            writes=[b_lutb], dma=True)
        p.add("gpsimd", lambda e: e.dma_start(cosq[:, :], cosq_d[:, :]),
              writes=[b_cosq], dma=True)
        p.add("gpsimd", lambda e: e.dma_start(sinq[:, :], sinq_d[:, :]),
              writes=[b_sinq], dma=True)
        p.add("gpsimd", lambda e: e.dma_start(perm[:, :], perm_d[:, :]),
              writes=[b_perm], dma=True)
        p.add("gpsimd", lambda e: e.dma_start(umask[:, :], um_d[:, :]),
              writes=[b_umask], dma=True)
        p.add("gpsimd", lambda e: e.dma_start(ubar[:, :], ub_d[:, :]),
              writes=[b_ubar], dma=True)
        p.add("vector", lambda e: e.memset(onescol[:, :], 1.0), writes=[b_onescol])
        p.add("vector", lambda e: e.memset(onesrow[:, :], 1.0), writes=[b_onesrow])
        p.add("vector", lambda e: e.memset(onestile[:, :], 1.0), writes=[b_onestile])

        wchunk_bufs = {}
        sel_jobs = ([(gidxT, wg_d, c, INTER) for c in range(NK)] +
                    [(uidxT, wu_d, c, INTER) for c in range(NK)])
        sel_jobs_d = [(didxT, wd_d, c, D) for c in range(KI)]
        if upto < 2:
            sel_jobs = []
            sel_jobs_d = []

        def make_sel_feeder(ph, jobs, width_max):
            six = [sball(ph, "sel_ix", [128, width_max], I32) for _ in range(2)]
            b_six = [Buf("sel_ix") for _ in range(2)]
            sixf = [sball(ph, "sel_ixf", [128, width_max], BF16)
                    for _ in range(2)]
            b_sixf = [Buf("sel_ixf") for _ in range(2)]
            sterm = sball(ph, "sel_term", [128, width_max], BF16)
            b_sterm = Buf("sel_term")
            sacc = [sball(ph, "sel_acc", [128, width_max], BF16)
                    for _ in range(2)]
            b_sacc = [Buf("sel_acc") for _ in range(2)]

            def _gen():
                for jn, (idx_dram, w_dram, c, width) in enumerate(jobs):
                    par = jn % 2
                    r0 = c * 128
                    p.add("sync", lambda e, s=idx_dram, r=r0, w=width,
                          t=six[par]: e.dma_start(t[:, 0:w], s[r:r + 128, :]),
                          writes=[b_six[par]], dma=True)
                    p.add("vector", lambda e, w=width, t=sixf[par],
                          s=six[par]: e.tensor_copy(t[:, 0:w], s[:, 0:w]),
                          reads=[b_six[par]], writes=[b_sixf[par]])
                    yield
                    p.add("vector", lambda e, w=width, t=sacc[par],
                          s=sixf[par]: e.tensor_scalar(
                              t[:, 0:w], s[:, 0:w], 0.0,
                              lutb[:, 0:1], ALU.is_equal, ALU.mult),
                          reads=[b_sixf[par], b_lutb], writes=[b_sacc[par]])
                    yield
                    for j in range(1, 16):
                        p.add("vector", lambda e, w=width, s=sixf[par], j=j:
                              e.tensor_scalar(
                                  sterm[:, 0:w], s[:, 0:w], float(j),
                                  lutb[:, j:j + 1], ALU.is_equal, ALU.mult),
                              reads=[b_sixf[par], b_lutb], writes=[b_sterm])
                        yield
                        p.add("vector", lambda e, w=width, t=sacc[par]:
                              e.tensor_tensor(t[:, 0:w], t[:, 0:w],
                                              sterm[:, 0:w], ALU.add),
                              reads=[b_sterm], writes=[b_sacc[par]])
                        yield
                    wb = Buf(f"wch_{w_dram.name}_{c}")
                    wchunk_bufs[(w_dram.name, c)] = wb
                    p.add("gpsimd", lambda e, d=w_dram, r=r0, w=width,
                          t=sacc[par]: e.dma_start(d[r:r + 128, :], t[:, 0:w]),
                          reads=[b_sacc[par]], writes=[wb], dma=True,
                          store=True)
            g = _gen()

            def step(n=1):
                for _ in range(n):
                    if next(g, "end") == "end":
                        break
            return step

        def ln_tmps(stk, tag):
            rb = [sball(stk, f"{tag}_rb", [128, 512], BF16) for _ in range(NK)]
            b_rb = [Buf(f"{tag}_rb{k}") for k in range(NK)]
            sqt = [sball(stk, f"{tag}_sq", [128, 512], BF16) for _ in range(2)]
            b_sqt = [Buf(f"{tag}_sq{i}") for i in range(2)]
            psm = psall(stk, f"{tag}_psm", [1, 512])
            psv = psall(stk, f"{tag}_psv", [1, 512])
            psb_ = psall(stk, f"{tag}_psb", [128, 1024])
            srow = sball(stk, f"{tag}_srow", [1, 2048], F32)
            bcb0 = sball(stk, f"{tag}_bcb0", [1, 512], BF16)
            bcb1 = sball(stk, f"{tag}_bcb1", [1, 512], BF16)
            bcb2 = sball(stk, f"{tag}_bcb2", [128, 1024], BF16)
            xtmp = [sball(stk, f"{tag}_xt", [128, 512], BF16) for _ in range(2)]
            b_xtmp = [Buf(f"{tag}_xt{i}") for i in range(2)]
            return dict(
                rb=rb, b_rb=b_rb, sqt=sqt, b_sqt=b_sqt,
                psm=psm, b_psm=Buf(f"{tag}_psm"),
                psv=psv, b_psv=Buf(f"{tag}_psv"),
                psb=psb_, b_psb=Buf(f"{tag}_psb"),
                srow=srow, b_srow=Buf(f"{tag}_srow"),
                bcb0=bcb0, b_bcb0=Buf(f"{tag}_b0"),
                bcb1=bcb1, b_bcb1=Buf(f"{tag}_b1"),
                bcb2=bcb2, b_bcb2=Buf(f"{tag}_b2"),
                xtmp=xtmp, b_xtmp=b_xtmp,
            )

        def layernorm_tg(src_dram, b_src, tg0, xn_tiles, b_xn, xn_col0, T):
            rb, b_rb = T["rb"], T["b_rb"]
            for k in range(NK):
                p.add("gpsimd", lambda e, k=k: e.dma_start(
                    rb[k][:, :], src_dram[k * 128:(k + 1) * 128, tg0:tg0 + 512]),
                    reads=[b_src], writes=[b_rb[k]], dma=True)
            for k in range(NK):
                p.add("tensor", lambda e, k=k: e.matmul(
                    T["psm"][0:1, :], onescol[:, :], rb[k][:, :],
                    start=(k == 0), stop=(k == NK - 1)),
                    reads=[b_rb[k], b_onescol], writes=[T["b_psm"]])
                par = k % 2
                p.add("scalar", lambda e, k=k, par=par: e.activation(
                    T["sqt"][par][:, :], rb[k][:, :], AF.Square),
                    reads=[b_rb[k]], writes=[T["b_sqt"][par]])
                p.add("tensor", lambda e, k=k, par=par: e.matmul(
                    T["psv"][0:1, :], onescol[:, :], T["sqt"][par][:, :],
                    start=(k == 0), stop=(k == NK - 1)),
                    reads=[T["b_sqt"][par], b_onescol], writes=[T["b_psv"]])
            sr = T["srow"]
            b_sr = T["b_srow"]
            p.add("vector", lambda e: e.tensor_scalar(
                sr[0:1, 0:512], T["psm"][0:1, :], 1.0 / D, None, ALU.mult),
                reads=[T["b_psm"]], writes=[b_sr])
            p.add("vector", lambda e: e.tensor_scalar(
                sr[0:1, 512:1024], T["psv"][0:1, :], 1.0 / D, None, ALU.mult),
                reads=[T["b_psv"]], writes=[b_sr])
            p.add("vector", lambda e: e.tensor_tensor(
                sr[0:1, 1024:1536], sr[0:1, 0:512], sr[0:1, 0:512], ALU.mult),
                writes=[b_sr])
            p.add("vector", lambda e: e.tensor_tensor(
                sr[0:1, 512:1024], sr[0:1, 512:1024], sr[0:1, 1024:1536],
                ALU.subtract), writes=[b_sr])
            p.add("scalar", lambda e: e.activation(
                sr[0:1, 1024:1536], sr[0:1, 512:1024], AF.Sqrt, bias=C_EPS),
                reads=[b_consts], writes=[b_sr])
            p.add("vector", lambda e: e.reciprocal(
                sr[0:1, 1536:2048], sr[0:1, 1024:1536]), writes=[b_sr])
            p.add("vector", lambda e: e.tensor_copy(
                T["bcb0"][0:1, :], sr[0:1, 0:512]),
                reads=[b_sr], writes=[T["b_bcb0"]])
            p.add("vector", lambda e: e.tensor_copy(
                T["bcb1"][0:1, :], sr[0:1, 1536:2048]),
                reads=[b_sr], writes=[T["b_bcb1"]])
            p.add("tensor", lambda e: e.matmul(
                T["psb"][:, 0:512], onesrow[:, :], T["bcb0"][0:1, :],
                start=True, stop=True),
                reads=[T["b_bcb0"], b_onesrow], writes=[T["b_psb"]])
            p.add("tensor", lambda e: e.matmul(
                T["psb"][:, 512:1024], onesrow[:, :], T["bcb1"][0:1, :],
                start=True, stop=True),
                reads=[T["b_bcb1"], b_onesrow], writes=[T["b_psb"]])
            p.add("scalar", lambda e: e.activation(
                T["bcb2"][:, :], T["psb"][:, :], AF.Copy),
                reads=[T["b_psb"]], writes=[T["b_bcb2"]])
            for k in range(NK):
                par = k % 2
                p.add("vector", lambda e, k=k, par=par: e.tensor_tensor(
                    T["xtmp"][par][:, :], rb[k][:, :], T["bcb2"][:, 0:512],
                    ALU.subtract),
                    reads=[b_rb[k], T["b_bcb2"]], writes=[T["b_xtmp"][par]])
                p.add("vector", lambda e, k=k, par=par: e.tensor_tensor(
                    xn_tiles[k][:, xn_col0:xn_col0 + 512], T["xtmp"][par][:, :],
                    T["bcb2"][:, 512:1024], ALU.mult),
                    reads=[T["b_xtmp"][par], T["b_bcb2"]], writes=[b_xn[k]])

        b_qc = Buf("qc_dram")
        b_qr = Buf("qr_dram")
        b_kc = Buf("kc_dram")
        b_v = Buf("v_dram")
        with contextlib.ExitStack() as phA:
            cq = [sball(phA, "a_cq", [128, S], BF16) for _ in range(NL)]
            b_cq = [Buf(f"a_cq{k}") for k in range(NL)]
            ckv = [sball(phA, "a_ckv", [128, S], BF16) for _ in range(NL)]
            b_ckv = [Buf(f"a_ckv{k}") for k in range(NL)]

            with contextlib.ExitStack() as ph:
                wdq = [sball(ph, "a_wdq", [128, DL], BF16) for _ in range(NK)]
                b_wdq = [Buf(f"a_wdq{k}") for k in range(NK)]
                wdkv = [sball(ph, "a_wdkv", [128, DL], BF16) for _ in range(NK)]
                b_wdkv = [Buf(f"a_wdkv{k}") for k in range(NK)]
                wkr = [sball(ph, "a_wkr", [128, DHR], BF16) for _ in range(NK)]
                b_wkr = [Buf(f"a_wkr{k}") for k in range(NK)]
                for k in range(NK):
                    p.add("gpsimd", lambda e, k=k: e.dma_start(
                        wdq[k][:, :], wdqT[k * 128:(k + 1) * 128, :]),
                        writes=[b_wdq[k]], dma=True)
                    p.add("gpsimd", lambda e, k=k: e.dma_start(
                        wdkv[k][:, :], wdkvT[k * 128:(k + 1) * 128, :]),
                        writes=[b_wdkv[k]], dma=True)
                    p.add("gpsimd", lambda e, k=k: e.dma_start(
                        wkr[k][:, :], wkrT[k * 128:(k + 1) * 128, :]),
                        writes=[b_wkr[k]], dma=True)

                xn = [[sball(ph, "a_xn", [128, 512], BF16) for _ in range(NK)]
                      for _ in range(2)]
                b_xn = [[Buf(f"a_xn{i}_{k}") for k in range(NK)] for i in range(2)]
                T = ln_tmps(ph, "aln")
                psp = [psall(ph, "a_psp", [128, 512]) for _ in range(2)]
                b_psp = [Buf(f"a_psp{i}") for i in range(2)]
                pskr = psall(ph, "a_pskr", [64, 512])
                b_pskr = Buf("a_pskr")
                pswp = psall(ph, "a_pswp", [64, 512])
                b_pswp = Buf("a_pswp")
                krr = [sball(ph, "a_krr", [64, 512], BF16) for _ in range(4)]
                b_krr = [Buf(f"a_krr{i}") for i in range(4)]
                b_xin = Buf("x_input")

                for tg in range(8 if upto >= 3 else 0):
                    ring = tg % 2
                    layernorm_tg(x_feat, b_xin, tg * 512, xn[ring], b_xn[ring], 0, T)
                    for which, wt, b_wt, dst, b_dst in (
                            (0, wdq, b_wdq, cq, b_cq),
                            (1, wdkv, b_wdkv, ckv, b_ckv)):
                        for do in range(NL):
                            i = (which * NL + do) % 2
                            ps, bps = psp[i], b_psp[i]

                            def mmgrp(e, wt=wt, do=do, ring=ring, ps=ps):
                                last = None
                                for k in range(NK):
                                    last = e.matmul(
                                        ps[:, :],
                                        wt[k][:, do * 128:(do + 1) * 128],
                                        xn[ring][k][:, :],
                                        start=(k == 0), stop=(k == NK - 1))
                                return last
                            p.add("tensor", mmgrp,
                                  reads=[b_wt[k] for k in range(NK)] + b_xn[ring],
                                  writes=[bps])
                            p.add("scalar",
                                  lambda e, dst=dst, do=do, tg=tg, ps=ps:
                                  e.activation(
                                      dst[do][:, tg * 512:(tg + 1) * 512],
                                      ps[:, :], AF.Copy),
                                  reads=[bps], writes=[b_dst[do]])

                    def krgrp(e, ring=ring):
                        last = None
                        for k in range(NK):
                            last = e.matmul(pskr[:, :], wkr[k][:, :],
                                            xn[ring][k][:, :],
                                            start=(k == 0), stop=(k == NK - 1))
                        return last
                    p.add("tensor", krgrp,
                          reads=[b_wkr[k] for k in range(NK)] + b_xn[ring],
                          writes=[b_pskr])
                    ia = 2 * (tg % 2)
                    kr_raw, b_raw = krr[ia], b_krr[ia]
                    kr_t1, b_t1 = krr[ia + 1], b_krr[ia + 1]
                    p.add("scalar", lambda e, t=kr_raw: e.activation(
                        t[:, :], pskr[:, :], AF.Copy),
                        reads=[b_pskr], writes=[b_raw])
                    p.add("tensor", lambda e, t=kr_raw: e.matmul(
                        pswp[:, :], perm[0:64, 0:64], t[:, :],
                        start=True, stop=True),
                        reads=[b_raw, b_perm], writes=[b_pswp])
                    p.add("vector", lambda e, t=kr_raw, o=kr_t1, tg=tg:
                          e.tensor_tensor(o[:, :], t[:, :],
                                          cosq[0:64, tg * 512:(tg + 1) * 512],
                                          ALU.mult),
                          reads=[b_raw, b_cosq], writes=[b_t1])
                    p.add("vector", lambda e, t=kr_raw, tg=tg: e.tensor_tensor(
                        t[:, :], pswp[:, :],
                        sinq[0:64, tg * 512:(tg + 1) * 512], ALU.mult),
                        reads=[b_pswp, b_sinq], writes=[b_raw])
                    p.add("vector", lambda e, t=kr_raw, o=kr_t1, tg=tg:
                          e.tensor_tensor(krsb[:, tg * 512:(tg + 1) * 512],
                                          o[:, :], t[:, :], ALU.add),
                          reads=[b_raw, b_t1], writes=[b_krsb])
                p.barrier()

            with contextlib.ExitStack() as ph:
                wuq = [sball(ph, "a2_wuq", [128, D], BF16) for _ in range(NL)]
                b_wuq = [Buf(f"a2_wuq{k}") for k in range(NL)]
                wqr = [sball(ph, "a2_wqr", [128, H * DHR], BF16) for _ in range(NL)]
                b_wqr = [Buf(f"a2_wqr{k}") for k in range(NL)]
                wuk = [sball(ph, "a2_wuk", [128, D], BF16) for _ in range(NL)]
                b_wuk = [Buf(f"a2_wuk{k}") for k in range(NL)]
                wuv = [sball(ph, "a2_wuv", [128, D], BF16) for _ in range(NL)]
                b_wuv = [Buf(f"a2_wuv{k}") for k in range(NL)]
                for k in range(NL if upto >= 4 else 0):
                    for wt, b_wt, srcd in ((wuq, b_wuq, wuqT), (wqr, b_wqr, wqrT),
                                           (wuk, b_wuk, wukT), (wuv, b_wuv, wuvT)):
                        p.add("gpsimd", lambda e, k=k, wt=wt, srcd=srcd:
                              e.dma_start(wt[k][:, :],
                                          srcd[k * 128:(k + 1) * 128, :]),
                              writes=[b_wt[k]], dma=True)

                psq = [psall(ph, "a2_ps", [128, 512]) for _ in range(4)]
                b_psq = [Buf(f"a2_ps{i}") for i in range(4)]
                pswq = psall(ph, "a2_pswq", [128, 512])
                b_pswq = Buf("a2_pswq")
                ev = [sball(ph, "a2_ev", [128, 512], BF16) for _ in range(6)]
                b_ev = [Buf(f"a2_ev{i}") for i in range(6)]

                un = [0]
                for wt, b_wt, src, b_src_l, dst_d, b_dst in (() if upto < 4 else (
                        (wuq, b_wuq, cq, b_cq, qc_d, b_qc),
                        (wuk, b_wuk, ckv, b_ckv, kc_d, b_kc))):
                    for do in range(NK):
                        for sub in range(8):
                            i = un[0] % 4
                            ps, bps = psq[i], b_psq[i]
                            et, bet = ev[un[0] % 6], b_ev[un[0] % 6]
                            un[0] += 1

                            def mmg(e, wt=wt, src=src, do=do, sub=sub, ps=ps):
                                last = None
                                for k in range(NL):
                                    last = e.matmul(
                                        ps[:, :],
                                        wt[k][:, do * 128:(do + 1) * 128],
                                        src[k][:, sub * 512:(sub + 1) * 512],
                                        start=(k == 0), stop=(k == NL - 1))
                                return last
                            p.add("tensor", mmg,
                                  reads=[b_wt[k] for k in range(NL)] + b_src_l,
                                  writes=[bps])
                            p.add("scalar", lambda e, et=et, ps=ps: e.activation(
                                et[:, :], ps[:, :], AF.Copy),
                                reads=[bps], writes=[bet])
                            p.add("gpsimd", lambda e, et=et, dst_d=dst_d, do=do,
                                  sub=sub: e.dma_start(
                                      dst_d[do * 128:(do + 1) * 128,
                                            sub * 512:(sub + 1) * 512], et[:, :]),
                                  reads=[bet], writes=[b_dst], dma=True,
                                  store=True)

                for do in range(H * DHR // 128 if upto >= 4 else 0):
                    for sub in range(8):
                        i = un[0] % 4
                        ps, bps = psq[i], b_psq[i]
                        raw, braw = ev[un[0] % 6], b_ev[un[0] % 6]
                        t1, bt1 = ev[(un[0] + 1) % 6], b_ev[(un[0] + 1) % 6]
                        un[0] += 2

                        def mmg(e, do=do, sub=sub, ps=ps):
                            last = None
                            for k in range(NL):
                                last = e.matmul(
                                    ps[:, :], wqr[k][:, do * 128:(do + 1) * 128],
                                    cq[k][:, sub * 512:(sub + 1) * 512],
                                    start=(k == 0), stop=(k == NL - 1))
                            return last
                        p.add("tensor", mmg,
                              reads=[b_wqr[k] for k in range(NL)] + b_cq,
                              writes=[bps])
                        p.add("scalar", lambda e, raw=raw, ps=ps: e.activation(
                            raw[:, :], ps[:, :], AF.Copy), reads=[bps],
                            writes=[braw])
                        p.add("tensor", lambda e, raw=raw: e.matmul(
                            pswq[:, :], perm[:, :], raw[:, :],
                            start=True, stop=True),
                            reads=[braw, b_perm], writes=[b_pswq])
                        p.add("vector", lambda e, raw=raw, t1=t1, sub=sub:
                              e.tensor_tensor(
                                  t1[:, :], raw[:, :],
                                  cosq[:, sub * 512:(sub + 1) * 512], ALU.mult),
                              reads=[braw, b_cosq], writes=[bt1])
                        p.add("vector", lambda e, raw=raw, sub=sub:
                              e.tensor_tensor(
                                  raw[:, :], pswq[:, :],
                                  sinq[:, sub * 512:(sub + 1) * 512], ALU.mult),
                              reads=[b_pswq, b_sinq], writes=[braw])
                        p.add("vector", lambda e, raw=raw, t1=t1: e.tensor_tensor(
                            t1[:, :], t1[:, :], raw[:, :], ALU.add),
                            reads=[braw], writes=[bt1])
                        p.add("gpsimd", lambda e, t1=t1, do=do, sub=sub:
                              e.dma_start(
                                  qr_d[do * 128:(do + 1) * 128,
                                       sub * 512:(sub + 1) * 512], t1[:, :]),
                              reads=[bt1], writes=[b_qr], dma=True, store=True)

                for tc in range(NT if upto >= 4 else 0):
                    for no in range(4):
                        i = un[0] % 4
                        ps, bps = psq[i], b_psq[i]
                        et, bet = ev[un[0] % 6], b_ev[un[0] % 6]
                        un[0] += 1

                        def mmg(e, tc=tc, no=no, ps=ps):
                            last = None
                            for k in range(NL):
                                last = e.matmul(
                                    ps[:, :], ckv[k][:, tc * 128:(tc + 1) * 128],
                                    wuv[k][:, no * 512:(no + 1) * 512],
                                    start=(k == 0), stop=(k == NL - 1))
                            return last
                        p.add("tensor", mmg,
                              reads=[b_wuv[k] for k in range(NL)] + b_ckv,
                              writes=[bps])
                        p.add("scalar", lambda e, et=et, ps=ps: e.activation(
                            et[:, :], ps[:, :], AF.Copy), reads=[bps],
                            writes=[bet])
                        p.add("gpsimd", lambda e, et=et, tc=tc, no=no:
                              e.dma_start(
                                  v_d[tc * 128:(tc + 1) * 128,
                                      no * 512:(no + 1) * 512], et[:, :]),
                              reads=[bet], writes=[b_v], dma=True, store=True)
                p.barrier()

        def attention(mla, qd, b_qd, qrd, b_qrd, kd, b_kd, vd, b_vd, aod,
                      b_aod, sel=None):
            with contextlib.ExitStack() as ph:
                ksb = [sball(ph, "at_k", [128, S], BF16) for _ in range(2)]
                b_ksb = [Buf(f"at_k{i}") for i in range(2)]
                vsb = [sball(ph, "at_v", [128, S], BF16) for _ in range(2)]
                b_vsb = [Buf(f"at_v{i}") for i in range(2)]
                qcs = [sball(ph, "at_qc", [128, QG], BF16) for _ in range(2)]
                b_qcs = [Buf(f"at_qc{i}") for i in range(2)]
                qrs = [sball(ph, "at_qr", [64, QG], BF16) for _ in range(2)]
                b_qrs = [Buf(f"at_qr{i}") for i in range(2)]
                psS = [psall(ph, "at_S", [128, QG]) for _ in range(2)]
                b_psS = [Buf(f"at_S{i}") for i in range(2)]
                psO = psall(ph, "at_O", [128, QG])
                b_psO = Buf("at_O")
                psA = psall(ph, "at_A", [128, QG])
                b_psA = Buf("at_A")
                pt = [sball(ph, "at_p", [128, QG], BF16) for _ in range(2)]
                b_pt = [Buf(f"at_p{i}") for i in range(2)]
                rr = sball(ph, "at_rr", [1, 2 * QG], F32)
                b_rr = Buf("at_rr")
                rrb = sball(ph, "at_rrb", [1, QG], BF16)
                b_rrb = Buf("at_rrb")
                bcb = [sball(ph, "at_bc", [128, QG], BF16) for _ in range(2)]
                b_bcb = [Buf(f"at_bc{i}") for i in range(2)]
                ob = [sball(ph, "at_ob", [128, QG], BF16) for _ in range(2)]
                b_ob = [Buf(f"at_ob{i}") for i in range(2)]
                suf = sball(ph, "at_suf", [128, 3], F32)
                b_suf = Buf("at_suf")

                if sel:
                    sel_step = make_sel_feeder(ph, sel, INTER)
                else:
                    def sel_step(n=1):
                        pass

                scale_ap = C_SMLA if mla else C_SMHA

                for h in range(H):
                    hp = h % 2
                    p.add("sync", lambda e, h=h, t=ksb[hp]: e.dma_start(
                        t[:, :], kd[h * 128:(h + 1) * 128, :]),
                        reads=[b_kd], writes=[b_ksb[hp]], dma=True)
                    for g8 in range(4):
                        p.add("sync", lambda e, h=h, g8=g8, t=vsb[hp]:
                              e.dma_start(
                                  bass.AP(t, g8 * 1024,
                                          [[S, 128], [128, 8], [1, 128]]),
                                  bass.AP(vd, g8 * 8 * 128 * D + h * 128,
                                          [[D, 128], [128 * D, 8], [1, 128]])),
                              reads=[b_vd], writes=[b_vsb[hp]], dma=True)
                    if not mla:
                        def sufmm(e, hp=hp, lo=None, hi=None, first=False,
                                  last=False):
                            pass
                        for gi, (lo, hi) in enumerate(((24, 32), (16, 24),
                                                       (8, 16))):
                            def fn(e, hp=hp, lo=lo, hi=hi, first=(gi == 0),
                                   last=(gi == 2)):
                                lastm = None
                                for kc in range(hi - 1, lo - 1, -1):
                                    lastm = e.matmul(
                                        psA[:, 0:1],
                                        vsb[hp][:, kc * 128:(kc + 1) * 128],
                                        onescol[:, :],
                                        start=(first and kc == hi - 1),
                                        stop=(last and kc == lo))
                                return lastm
                            p.add("tensor", fn,
                                  reads=[b_vsb[hp], b_onescol],
                                  writes=[b_psA])
                            p.add("scalar", lambda e, gi=gi: e.activation(
                                suf[:, 2 - gi:3 - gi], psA[:, 0:1], AF.Copy),
                                reads=[b_psA], writes=[b_suf])
                    for g in range(NQG):
                        gp = g % 2
                        q0 = g * QG
                        p.add("sync", lambda e, h=h, q0=q0, t=qcs[gp]:
                              e.dma_start(t[:, :],
                                          qd[h * 128:(h + 1) * 128, q0:q0 + QG]),
                              reads=[b_qd], writes=[b_qcs[gp]], dma=True)
                        if mla:
                            p.add("sync", lambda e, h=h, q0=q0, t=qrs[gp]:
                                  e.dma_start(
                                      t[:, :],
                                      qrd[h * DHR:(h + 1) * DHR, q0:q0 + QG]),
                                  reads=[b_qrd], writes=[b_qrs[gp]], dma=True)
                        kc_hi = (g + 1) * 8
                        nkc_total = kc_hi

                        def dp(kc):
                            par = kc % 2

                            def fn(e, kc=kc, par=par, hp=hp,
                                   start=(kc == 0), stop=(kc == nkc_total - 1)):
                                last = None
                                for half in range(2):
                                    c0 = half * 512
                                    rhs = pt[par][:, c0:c0 + 512]
                                    e.matmul(psA[0:1, c0:c0 + 512],
                                             onescol[:, :], rhs,
                                             start=start, stop=stop)
                                    last = e.matmul(
                                        psO[:, c0:c0 + 512],
                                        vsb[hp][:, kc * 128:(kc + 1) * 128],
                                        rhs, start=start, stop=stop)
                                return last
                            p.add("tensor", fn,
                                  reads=[b_pt[par], b_vsb[hp], b_onescol],
                                  writes=[b_psO, b_psA])

                        pending = None
                        for kc in range(kc_hi):
                            par = kc % 2

                            def smm(e, kc=kc, par=par, gp=gp, hp=hp):
                                last = None
                                for half in range(2):
                                    c0 = half * 512
                                    if mla:
                                        e.matmul(psS[par][:, c0:c0 + 512],
                                                 ksb[hp][:, kc * 128:(kc + 1) * 128],
                                                 qcs[gp][:, c0:c0 + 512],
                                                 start=True, stop=False)
                                        last = e.matmul(
                                            psS[par][:, c0:c0 + 512],
                                            krsb[:, kc * 128:(kc + 1) * 128],
                                            qrs[gp][:, c0:c0 + 512],
                                            start=False, stop=True)
                                    else:
                                        last = e.matmul(
                                            psS[par][:, c0:c0 + 512],
                                            ksb[hp][:, kc * 128:(kc + 1) * 128],
                                            qcs[gp][:, c0:c0 + 512],
                                            start=True, stop=True)
                                return last
                            reads = [b_ksb[hp], b_qcs[gp]]
                            if mla:
                                reads += [b_krsb, b_qrs[gp]]
                            p.add("tensor", smm, reads=reads, writes=[b_psS[par]])
                            p.add("scalar", lambda e, par=par: e.activation(
                                pt[par][:, :], psS[par][:, :], AF.Exp,
                                scale=scale_ap),
                                reads=[b_psS[par], b_consts], writes=[b_pt[par]])
                            if mla:
                                p.add("vector", lambda e, par=par:
                                      e.tensor_scalar(
                                          pt[par][:, :], pt[par][:, :],
                                          CLAMP_HI, CLAMP_LO,
                                          ALU.min, ALU.max),
                                      writes=[b_pt[par]])
                            if kc * 128 >= q0:
                                w = kc * 128 - q0
                                if w > 0:
                                    fv = 0.0 if mla else 1.0
                                    p.add("vector", lambda e, par=par, w=w,
                                          fv=fv: e.memset(pt[par][:, 0:w], fv),
                                          writes=[b_pt[par]])
                                p.add("vector", lambda e, par=par, w=w:
                                      e.tensor_tensor(pt[par][:, w:w + 128],
                                                      pt[par][:, w:w + 128],
                                                      umask[:, :], ALU.mult),
                                      reads=[b_umask], writes=[b_pt[par]])
                                if not mla:
                                    p.add("vector", lambda e, par=par, w=w:
                                          e.tensor_tensor(pt[par][:, w:w + 128],
                                                          pt[par][:, w:w + 128],
                                                          ubar[:, :], ALU.add),
                                          reads=[b_ubar], writes=[b_pt[par]])
                            if pending is not None:
                                dp(pending)
                            pending = kc
                            sel_step(2)
                        if pending is not None:
                            dp(pending)
                        cg = 0.0 if mla else float(S - (g + 1) * QG)
                        p.add("vector", lambda e, cg=cg: e.tensor_scalar(
                            rr[0:1, QG:2 * QG], psA[0:1, 0:QG], cg, None,
                            ALU.add), reads=[b_psA], writes=[b_rr])
                        p.add("vector", lambda e: e.reciprocal(
                            rr[0:1, 0:QG], rr[0:1, QG:2 * QG]),
                            writes=[b_rr])
                        p.add("vector", lambda e: e.tensor_copy(
                            rrb[0:1, :], rr[0:1, 0:QG]),
                            reads=[b_rr], writes=[b_rrb])

                        def bcmm(e):
                            last = None
                            for half in range(2):
                                c0 = half * 512
                                last = e.matmul(psA[:, c0:c0 + 512],
                                                onesrow[:, :],
                                                rrb[0:1, c0:c0 + 512],
                                                start=True, stop=True)
                            return last
                        p.add("tensor", bcmm, reads=[b_rrb, b_onesrow],
                              writes=[b_psA])
                        p.add("scalar", lambda e, gp=gp: e.activation(
                            bcb[gp][:, :], psA[:, :], AF.Copy),
                            reads=[b_psA], writes=[b_bcb[gp]])
                        if mla or g == NQG - 1:
                            p.add("vector", lambda e, gp=gp: e.tensor_tensor(
                                ob[gp][:, :], psO[:, :], bcb[gp][:, :],
                                ALU.mult),
                                reads=[b_psO, b_bcb[gp]], writes=[b_ob[gp]])
                        else:
                            p.add("vector", lambda e, gp=gp, g=g:
                                  e.scalar_tensor_tensor(
                                      ob[gp][:, :], psO[:, :],
                                      suf[:, g:g + 1], bcb[gp][:, :],
                                      ALU.add, ALU.mult),
                                  reads=[b_psO, b_bcb[gp], b_suf],
                                  writes=[b_ob[gp]])
                        p.add("gpsimd", lambda e, h=h, q0=q0, gp=gp:
                              e.dma_start(
                                  aod[h * 128:(h + 1) * 128, q0:q0 + QG],
                                  ob[gp][:, :]),
                              reads=[b_ob[gp]], writes=[b_aod], dma=True,
                              store=True)
                sel_step(1 << 30)
                p.barrier()

        def out_proj(w_dram, aod, b_aod, resid_d, b_resid, out_d, b_out):
            with contextlib.ExitStack() as ph:
                wsb = [sball(ph, "op_w", [128, D], BF16) for _ in range(NK)]
                b_wsb = [Buf(f"op_w{k}") for k in range(NK)]
                aos = [sball(ph, "op_ao", [128, 2048], BF16) for _ in range(NK)]
                b_aos = [Buf(f"op_ao{k}") for k in range(NK)]
                ps = [psall(ph, "op_ps", [128, 512]) for _ in range(4)]
                b_ps = [Buf(f"op_ps{i}") for i in range(4)]
                rt = [sball(ph, "op_r", [128, 512], F32) for _ in range(4)]
                b_rt = [Buf(f"op_r{i}") for i in range(4)]
                rn = [sball(ph, "op_rn", [128, 512], F32) for _ in range(4)]
                b_rn = [Buf(f"op_rn{i}") for i in range(4)]

                for k in range(NK):
                    p.add("gpsimd", lambda e, k=k: e.dma_start(
                        wsb[k][:, :], w_dram[k * 128:(k + 1) * 128, :]),
                        writes=[b_wsb[k]], dma=True)
                un = 0
                for tb in range(2):
                    c0 = tb * 2048
                    for k in range(NK):
                        p.add("sync", lambda e, k=k, c0=c0: e.dma_start(
                            aos[k][:, :],
                            aod[k * 128:(k + 1) * 128, c0:c0 + 2048]),
                            reads=[b_aod], writes=[b_aos[k]], dma=True)
                    for do in range(NK):
                        for sub in range(4):
                            i = un % 4
                            un += 1
                            cc = c0 + sub * 512

                            def mmg(e, do=do, sub=sub, i=i):
                                last = None
                                for k in range(NK):
                                    last = e.matmul(
                                        ps[i][:, :],
                                        wsb[k][:, do * 128:(do + 1) * 128],
                                        aos[k][:, sub * 512:(sub + 1) * 512],
                                        start=(k == 0), stop=(k == NK - 1))
                                return last
                            p.add("tensor", mmg,
                                  reads=[b_wsb[k] for k in range(NK)] +
                                        [b_aos[k] for k in range(NK)],
                                  writes=[b_ps[i]])
                            p.add("sync", lambda e, do=do, cc=cc, i=i:
                                  e.dma_start(
                                      rt[i][:, :],
                                      resid_d[do * 128:(do + 1) * 128,
                                              cc:cc + 512]),
                                  reads=[b_resid], writes=[b_rt[i]], dma=True)
                            p.add("vector", lambda e, i=i: e.tensor_tensor(
                                rn[i][:, :], ps[i][:, :], rt[i][:, :], ALU.add),
                                reads=[b_ps[i], b_rt[i]], writes=[b_rn[i]])
                            p.add("gpsimd", lambda e, do=do, cc=cc, i=i:
                                  e.dma_start(
                                      out_d[do * 128:(do + 1) * 128, cc:cc + 512],
                                      rn[i][:, :]),
                                  reads=[b_rn[i]], writes=[b_out], dma=True,
                                  store=True)
                p.barrier()

        def ffn(resid_d, b_resid, out_d, b_out, sel=None):
            b_hp = [[Buf(f"hp_{c}_{t}") for t in range(8)] for c in range(KI)]
            with contextlib.ExitStack() as ph:
                xnb = [[sball(ph, "f_xn", [128, 1024], BF16)
                        for _ in range(NK)] for _ in range(2)]
                b_xnb = [[Buf(f"f_xn{r}_{k}") for k in range(NK)]
                         for r in range(2)]
                T = ln_tmps(ph, "fln")
                wgt = [sball(ph, "f_wg", [128, 2048], BF16) for _ in range(2)]
                b_wgt = [Buf(f"f_wg{i}") for i in range(2)]
                wut = [sball(ph, "f_wu", [128, 2048], BF16) for _ in range(2)]
                b_wut = [Buf(f"f_wu{i}") for i in range(2)]
                psg = [psall(ph, "f_pg", [128, 512]) for _ in range(2)]
                b_psg = [Buf(f"f_pg{i}") for i in range(2)]
                psu = [psall(ph, "f_pu", [128, 512]) for _ in range(2)]
                b_psu = [Buf(f"f_pu{i}") for i in range(2)]
                sg = [sball(ph, "f_sg", [128, 512], BF16) for _ in range(2)]
                b_sg = [Buf(f"f_sg{i}") for i in range(2)]
                hpt = [sball(ph, "f_hp", [128, 512], BF16) for _ in range(2)]
                b_hpt = [Buf(f"f_hp{i}") for i in range(2)]
                if sel:
                    fsel_step = make_sel_feeder(ph, sel, D)
                else:
                    def fsel_step(n=1):
                        pass

                for tb in range(4):
                    ring = tb % 2
                    c0 = tb * 1024
                    for tg in range(2):
                        layernorm_tg(resid_d, b_resid, c0 + tg * 512,
                                     xnb[ring], b_xnb[ring], tg * 512, T)
                    for do in range(KI):
                        dpar = do % 2
                        p.add("sync", lambda e, do=do, t=wgt[dpar]: e.dma_start(
                            bass.AP(t, 0, [[2048, 128], [128, NK], [1, 128]]),
                            bass.AP(wg_d, do * 128,
                                    [[INTER, 128], [128 * INTER, NK], [1, 128]])),
                            reads=[wchunk_bufs[("wg_d", k)] for k in range(NK)],
                            writes=[b_wgt[dpar]], dma=True)
                        p.add("sync", lambda e, do=do, t=wut[dpar]: e.dma_start(
                            bass.AP(t, 0, [[2048, 128], [128, NK], [1, 128]]),
                            bass.AP(wu_d, do * 128,
                                    [[INTER, 128], [128 * INTER, NK], [1, 128]])),
                            reads=[wchunk_bufs[("wu_d", k)] for k in range(NK)],
                            writes=[b_wut[dpar]], dma=True)
                        for sub in range(2):
                            spar = sub % 2

                            def gmm(e, dpar=dpar, spar=spar, sub=sub,
                                    ring=ring):
                                last = None
                                for k in range(NK):
                                    last = e.matmul(
                                        psg[spar][:, :],
                                        wgt[dpar][:, k * 128:(k + 1) * 128],
                                        xnb[ring][k][:, sub * 512:(sub + 1) * 512],
                                        start=(k == 0), stop=(k == NK - 1))
                                return last
                            p.add("tensor", gmm,
                                  reads=[b_wgt[dpar]] + b_xnb[ring],
                                  writes=[b_psg[spar]])

                            def umm(e, dpar=dpar, spar=spar, sub=sub,
                                    ring=ring):
                                last = None
                                for k in range(NK):
                                    last = e.matmul(
                                        psu[spar][:, :],
                                        wut[dpar][:, k * 128:(k + 1) * 128],
                                        xnb[ring][k][:, sub * 512:(sub + 1) * 512],
                                        start=(k == 0), stop=(k == NK - 1))
                                return last
                            p.add("tensor", umm,
                                  reads=[b_wut[dpar]] + b_xnb[ring],
                                  writes=[b_psu[spar]])
                            p.add("scalar", lambda e, spar=spar: e.activation(
                                sg[spar][:, :], psg[spar][:, :], AF.Silu),
                                reads=[b_psg[spar]], writes=[b_sg[spar]])
                            p.add("vector", lambda e, spar=spar: e.tensor_tensor(
                                hpt[spar][:, :], sg[spar][:, :],
                                psu[spar][:, :], ALU.mult),
                                reads=[b_sg[spar], b_psu[spar]],
                                writes=[b_hpt[spar]])
                            tcol = tb * 2 + sub
                            p.add("gpsimd", lambda e, do=do, c0=c0, sub=sub,
                                  spar=spar: e.dma_start(
                                      hp_d[do * 128:(do + 1) * 128,
                                           c0 + sub * 512:c0 + (sub + 1) * 512],
                                      hpt[spar][:, :]),
                                  reads=[b_hpt[spar]],
                                  writes=[b_hp[do][tcol]], dma=True, store=True)
                            fsel_step(5)
                fsel_step(1 << 30)
                p.barrier()
            with contextlib.ExitStack() as ph:
                hps = [sball(ph, "d_hp", [128, 1024], BF16) for _ in range(KI)]
                b_hps = [Buf(f"d_hp{k}") for k in range(KI)]
                wdt = [sball(ph, "d_wd", [128, INTER], BF16) for _ in range(2)]
                b_wdt = [Buf(f"d_wd{i}") for i in range(2)]
                psd = [psall(ph, "d_ps", [128, 512]) for _ in range(4)]
                b_psd = [Buf(f"d_ps{i}") for i in range(4)]
                rt = [sball(ph, "d_r", [128, 512], F32) for _ in range(4)]
                b_rt = [Buf(f"d_r{i}") for i in range(4)]
                rn = [sball(ph, "d_rn", [128, 512], F32) for _ in range(4)]
                b_rn = [Buf(f"d_rn{i}") for i in range(4)]

                for td in range(4):
                    c0 = td * 1024
                    tc0, tc1 = c0 // 512, c0 // 512 + 1
                    for k in range(KI):
                        p.add("sync", lambda e, k=k, c0=c0: e.dma_start(
                            hps[k][:, :],
                            hp_d[k * 128:(k + 1) * 128, c0:c0 + 1024]),
                            reads=[b_hp[k][tc0], b_hp[k][tc1]],
                            writes=[b_hps[k]], dma=True)
                    un = 0
                    for do in range(NK):
                        dpar = do % 2
                        for q in range(4):
                            klo = (KI * q) // 4
                            khi = (KI * (q + 1)) // 4
                            p.add("sync", lambda e, do=do, kl=klo, kh=khi,
                                  t=wdt[dpar]: e.dma_start(
                                      bass.AP(t, kl * 128,
                                              [[INTER, 128], [128, kh - kl],
                                               [1, 128]]),
                                      bass.AP(wd_d, kl * 128 * D + do * 128,
                                              [[D, 128], [128 * D, kh - kl],
                                               [1, 128]])),
                                  reads=[wchunk_bufs[("wd_d", k)]
                                         for k in range(klo, khi)],
                                  writes=[b_wdt[dpar]], dma=True)
                        for sub in range(2):
                            i = un % 4
                            un += 1
                            cc = c0 + sub * 512

                            def dmm(e, dpar=dpar, sub=sub, i=i):
                                last = None
                                for k in range(KI):
                                    last = e.matmul(
                                        psd[i][:, :],
                                        wdt[dpar][:, k * 128:(k + 1) * 128],
                                        hps[k][:, sub * 512:(sub + 1) * 512],
                                        start=(k == 0), stop=(k == KI - 1))
                                return last
                            p.add("tensor", dmm,
                                  reads=[b_wdt[dpar]] + b_hps,
                                  writes=[b_psd[i]])
                            p.add("sync", lambda e, do=do, cc=cc, i=i:
                                  e.dma_start(
                                      rt[i][:, :],
                                      resid_d[do * 128:(do + 1) * 128,
                                              cc:cc + 512]),
                                  reads=[b_resid], writes=[b_rt[i]], dma=True)
                            p.add("vector", lambda e, i=i: e.tensor_tensor(
                                rn[i][:, :], psd[i][:, :], rt[i][:, :], ALU.add),
                                reads=[b_psd[i], b_rt[i]], writes=[b_rn[i]])
                            p.add("gpsimd", lambda e, do=do, cc=cc, i=i:
                                  e.dma_start(
                                      out_d[do * 128:(do + 1) * 128,
                                            cc:cc + 512], rn[i][:, :]),
                                  reads=[b_rn[i]], writes=[b_out], dma=True,
                                  store=True)
            p.barrier()

        def mha_proj(resid_d, b_resid):
            b_q2 = Buf("q2_dram")
            b_k2 = Buf("k2_dram")
            b_v2 = Buf("v2_dram")
            with contextlib.ExitStack() as ph:
                xnb = [sball(ph, "m_xn", [128, 2048], BF16) for _ in range(NK)]
                b_xnb = [Buf(f"m_xn{k}") for k in range(NK)]
                T = ln_tmps(ph, "mln")
                wsb = [sball(ph, "m_w", [128, D], BF16) for _ in range(NK)]
                b_wsb = [Buf(f"m_w{k}") for k in range(NK)]
                ps = [psall(ph, "m_ps", [128, 512]) for _ in range(4)]
                b_ps = [Buf(f"m_ps{i}") for i in range(4)]
                ev = [sball(ph, "m_ev", [128, 512], BF16) for _ in range(4)]
                b_ev = [Buf(f"m_ev{i}") for i in range(4)]

                for tb in range(2):
                    c0 = tb * 2048
                    for tg in range(4):
                        layernorm_tg(resid_d, b_resid, c0 + tg * 512,
                                     xnb, b_xnb, tg * 512, T)
                    for w_dram, dst_d, b_dst, tokmajor in (
                            (wq, q2_d, b_q2, False), (wk, k2_d, b_k2, False),
                            (wv, v_d, b_v2, True)):
                        for k in range(NK):
                            p.add("gpsimd", lambda e, k=k, w=w_dram:
                                  e.dma_start(wsb[k][:, :],
                                              w[k * 128:(k + 1) * 128, :]),
                                  writes=[b_wsb[k]], dma=True)
                        un = 0
                        if not tokmajor:
                            for do in range(NK):
                                for sub in range(4):
                                    i = un % 4
                                    un += 1

                                    def mmg(e, do=do, sub=sub, i=i):
                                        last = None
                                        for k in range(NK):
                                            last = e.matmul(
                                                ps[i][:, :],
                                                wsb[k][:, do * 128:(do + 1) * 128],
                                                xnb[k][:,
                                                       sub * 512:(sub + 1) * 512],
                                                start=(k == 0),
                                                stop=(k == NK - 1))
                                        return last
                                    p.add("tensor", mmg,
                                          reads=[b_wsb[k] for k in range(NK)] +
                                                b_xnb,
                                          writes=[b_ps[i]])
                                    p.add("scalar", lambda e, i=i: e.activation(
                                        ev[i][:, :], ps[i][:, :], AF.Copy),
                                        reads=[b_ps[i]], writes=[b_ev[i]])
                                    p.add("gpsimd", lambda e, do=do, c0=c0,
                                          sub=sub, i=i, d=dst_d, b=b_dst:
                                          e.dma_start(
                                              d[do * 128:(do + 1) * 128,
                                                c0 + sub * 512:
                                                c0 + (sub + 1) * 512],
                                              ev[i][:, :]),
                                          reads=[b_ev[i]], writes=[b_dst],
                                          dma=True, store=True)
                        else:
                            for tc in range(16):
                                for no in range(4):
                                    i = un % 4
                                    un += 1

                                    def mmg(e, tc=tc, no=no, i=i):
                                        last = None
                                        for k in range(NK):
                                            last = e.matmul(
                                                ps[i][:, :],
                                                xnb[k][:, tc * 128:(tc + 1) * 128],
                                                wsb[k][:, no * 512:(no + 1) * 512],
                                                start=(k == 0),
                                                stop=(k == NK - 1))
                                        return last
                                    p.add("tensor", mmg,
                                          reads=[b_wsb[k] for k in range(NK)] +
                                                b_xnb,
                                          writes=[b_ps[i]])
                                    p.add("scalar", lambda e, i=i: e.activation(
                                        ev[i][:, :], ps[i][:, :], AF.Copy),
                                        reads=[b_ps[i]], writes=[b_ev[i]])
                                    gtc = tb * 16 + tc
                                    p.add("gpsimd", lambda e, gtc=gtc, no=no,
                                          i=i: e.dma_start(
                                              v_d[gtc * 128:(gtc + 1) * 128,
                                                  no * 512:(no + 1) * 512],
                                              ev[i][:, :]),
                                          reads=[b_ev[i]], writes=[b_v2],
                                          dma=True, store=True)
            p.barrier()
            return b_q2, b_k2, b_v2

        b_ao = Buf("ao_dram")
        b_r1 = Buf("r1")
        b_r2 = Buf("r2")
        b_r3 = Buf("r3")
        b_outd = Buf("outd")

        if upto >= 5:
            attention(True, qc_d, b_qc, qr_d, b_qr, kc_d, b_kc, v_d, b_v,
                      ao_d, b_ao, sel=sel_jobs)
        if upto >= 6:
            out_proj(woT, ao_d, b_ao, x_feat, Buf("x2"), r1_d, b_r1)
        if upto >= 7:
            ffn(r1_d, b_r1, r2_d, b_r2, sel=sel_jobs_d)
        if upto >= 8:
            bq2, bk2, bv2 = mha_proj(r2_d, b_r2)
        if upto >= 9:
            attention(False, q2_d, bq2, None, None, k2_d, bk2, v_d, bv2,
                      ao_d, b_ao)
        if upto >= 10:
            out_proj(wo, ao_d, b_ao, r2_d, b_r2, r3_d, b_r3)
        if upto >= 11:
            ffn(r3_d, b_r3, out_feat, b_outd)
        if dbg:
            dbg_map = {"qc": qc_d, "qr": qr_d, "kc": kc_d, "v": v_d,
                       "ao": ao_d, "r1": r1_d, "r2": r2_d, "r3": r3_d,
                       "hp": hp_d, "wg": wg_d, "wu": wu_d, "wd": wd_d,
                       "q2": q2_d, "k2": k2_d}
            for name in dbg:
                srct = dbg_map[name]
                od = nc.dram_tensor(f"dbg_{name}", list(srct.shape),
                                    srct.dtype, kind="ExternalOutput")
                p.add("sync", lambda e, od=od, s=srct: e.dma_start(
                    od[:, :], s[:, :]), dma=True, store=True)
        p.barrier()

        N_DMA_SEMS = 90
        p.finalize(N_DMA_SEMS)
        csems = {e: es.enter_context(nc.semaphore(f"c_{e}")) for e in ENGS}
        dsems = [es.enter_context(nc.semaphore(f"dch_{i}"))
                 for i in range(N_DMA_SEMS)]
        block = es.enter_context(nc.Block())
        p.emit(block, csems, dsems)

    return nc


_NC_CACHE = {}


def kernel(**inputs):
    x = np.asarray(inputs["L0_attn_a_x"], dtype=np.float32)
    n = 8

    def t(a):
        return np.ascontiguousarray(np.asarray(a, dtype=np.float32).T)

    consts = _host_constants()
    shared = {
        "wdqT": t(inputs["L0_attn_a_W_DQ"]),
        "wuqT": t(inputs["L0_attn_a_W_UQ"]),
        "wqrT": t(inputs["L0_attn_a_W_QR"]),
        "wdkvT": t(inputs["L0_attn_a_W_DKV"]),
        "wukT": t(inputs["L0_attn_a_W_UK"]),
        "wuvT": t(inputs["L0_attn_a_W_UV"]),
        "wkrT": t(inputs["L0_attn_a_W_KR"]),
        "woT": t(inputs["L0_attn_a_W_O"]),
        "wq": np.ascontiguousarray(np.asarray(inputs["L0_attn_b_Wq"],
                                              dtype=np.float32)),
        "wk": np.ascontiguousarray(np.asarray(inputs["L0_attn_b_Wk"],
                                              dtype=np.float32)),
        "wv": np.ascontiguousarray(np.asarray(inputs["L0_attn_b_Wv"],
                                              dtype=np.float32)),
        "wo": np.ascontiguousarray(np.asarray(inputs["L0_attn_b_Wo"],
                                              dtype=np.float32)),
        "gidxT": np.ascontiguousarray(
            np.asarray(inputs["L0_ffn_gate_idx"], dtype=np.int32).T),
        "uidxT": np.ascontiguousarray(
            np.asarray(inputs["L0_ffn_up_idx"], dtype=np.int32).T),
        "didxT": np.ascontiguousarray(
            np.asarray(inputs["L0_ffn_down_idx"], dtype=np.int32).T),
        "lutrow": np.asarray(inputs["L0_ffn_lut"],
                             dtype=np.float32).reshape(1, 16),
        "cosq": consts["cosq"],
        "sinq": consts["sinq"],
        "perm": consts["perm"],
        "umask": consts["umask"],
        "ubar": consts["ubar"],
        "consts": consts["consts"],
    }

    in_maps = []
    for c in range(n):
        m = dict(shared)
        m["x_feat"] = np.ascontiguousarray(x[c].T)
        in_maps.append(m)

    if "nc" not in _NC_CACHE:
        _NC_CACHE["nc"] = build_nc()
    nc = _NC_CACHE["nc"]

    res = run_bass_kernel_spmd(nc, in_maps, core_ids=list(range(n)))
    out = np.empty((n, S, D), dtype=np.float32)
    for c in range(n):
        out[c] = res.results[c]["out_feat"].T
    return out


# version 08
# speedup vs baseline: 1.2166x; 1.2166x over previous
import contextlib
import itertools
import math

import numpy as np
import ml_dtypes

import concourse.bass as bass
import concourse.mybir as mybir
from concourse.bass_utils import run_bass_kernel_spmd

F32 = mybir.dt.float32
BF16 = mybir.dt.bfloat16
I32 = mybir.dt.int32
AF = mybir.ActivationFunctionType
ALU = mybir.AluOpType

S = 4096
D = 2048
NK = D // 128
H = 16
DH = 128
DHR = 64
DL = 512
NL = DL // 128
INTER = 5504
KI = INTER // 128
NT = S // 128
QG = 1024
NQG = S // QG

SCALE_MLA = 1.0 / math.sqrt(DH + DHR)
SCALE_MHA = 1.0 / math.sqrt(S)
CLAMP_HI = float(math.exp(80.0 * SCALE_MLA))
CLAMP_LO = float(math.exp(-80.0 * SCALE_MLA))
EPS = 1e-5

ENGS = ("tensor", "scalar", "vector", "sync", "gpsimd")


class Op:
    __slots__ = ("eng", "fn", "deps", "dma", "marked", "val", "gid", "epoch",
                 "chan")

    def __init__(self, eng, fn, dma):
        self.eng = eng
        self.fn = fn
        self.deps = []
        self.dma = dma
        self.marked = dma
        self.val = None
        self.gid = None
        self.epoch = 0
        self.chan = None


class Buf:
    __slots__ = ("writer", "readers", "name")

    def __init__(self, name=""):
        self.writer = None
        self.readers = []
        self.name = name


class Chan:
    __slots__ = ("key", "ops", "sem_idx", "base", "is_store")

    def __init__(self, key, is_store):
        self.key = key
        self.ops = []
        self.sem_idx = None
        self.base = 0
        self.is_store = is_store


class Plan:
    def __init__(self):
        self.ops = {e: [] for e in ENGS}
        self.gseq = 0
        self.epoch = 0
        self.chans = {}
        self.dirty_chans = set()

    def _chan(self, key, is_store=False):
        ch = self.chans.get(key)
        if ch is None:
            ch = Chan(key, is_store)
            self.chans[key] = ch
        return ch

    def add(self, eng, fn, reads=(), writes=(), dma=False, store=False,
            extra_deps=()):
        op = Op(eng, fn, dma)
        dep = set()
        for b in reads:
            if b.writer is not None:
                dep.add(b.writer)
            b.readers.append(op)
        for b in writes:
            if b.writer is not None:
                dep.add(b.writer)
            dep.update(b.readers)
            b.writer = op
            b.readers = []
        for d in extra_deps:
            if d is not None:
                dep.add(d)
        dep.discard(op)
        op.deps = list(dep)
        op.gid = self.gseq
        op.epoch = self.epoch
        self.gseq += 1
        for d in op.deps:
            assert d.gid < op.gid, "wait on future op -> deadlock"
        self.ops[eng].append(op)
        if dma:
            if store:
                ch = (self._chan(id(reads[0])) if reads
                      else self._chan(("store", eng), True))
            else:
                assert writes, "dma load must declare its target buffer"
                ch = self._chan(id(writes[0]))
            ch.ops.append(op)
            op.chan = ch
            self.dirty_chans.add(ch.key)
        return op

    def barrier(self):
        tails = []
        for e in ENGS:
            for op in reversed(self.ops[e]):
                if op.fn is not None:
                    tails.append(op)
                    break
        for key in self.dirty_chans:
            tails.append(self.chans[key].ops[-1])
        self.dirty_chans = set()
        tails = list({id(t): t for t in tails}.values())
        for e in ENGS:
            self.add(e, None, extra_deps=tails)
        self.epoch += 1

    def finalize(self, n_dma_sems):
        for e in ENGS:
            for op in self.ops[e]:
                for d in op.deps:
                    if d.dma:
                        continue
                    if d.eng == "tensor" and op.eng == "tensor":
                        continue
                    d.marked = True
        for e in ENGS:
            c = 0
            for op in self.ops[e]:
                if (not op.dma) and op.marked:
                    c += 1
                    op.val = c
        store_chs = [ch for ch in self.chans.values() if ch.is_store]
        buf_chs = [ch for ch in self.chans.values() if not ch.is_store]
        n_pool = n_dma_sems - len(store_chs)
        assert n_pool > 0
        for i, ch in enumerate(store_chs):
            ch.sem_idx = n_pool + i
            ch.base = 0
        sem_busy_until = [-1] * n_pool
        sem_count = [0] * n_pool
        for ch in sorted(buf_chs, key=lambda c: c.ops[0].gid):
            e0 = ch.ops[0].epoch
            e1 = ch.ops[-1].epoch
            placed = False
            for s in range(n_pool):
                if sem_busy_until[s] < e0:
                    ch.sem_idx = s
                    ch.base = 16 * sem_count[s]
                    sem_count[s] += len(ch.ops)
                    sem_busy_until[s] = e1
                    placed = True
                    break
            assert placed, f"out of dma sems at epoch {e0}"
        for ch in self.chans.values():
            for i, op in enumerate(ch.ops):
                op.val = ch.base + 16 * (i + 1)
        self.n_store_chs = len(store_chs)

    def emit(self, block, csems, dsems):
        plan = self

        def mk(engname):
            def emit_fn(eng):
                hw = {}
                for op in plan.ops[engname]:
                    tgt = {}
                    for d in op.deps:
                        if d.val is None:
                            continue
                        if d.dma:
                            key = ("d", d.chan.sem_idx)
                        else:
                            if d.eng == "tensor" and engname == "tensor":
                                continue
                            key = ("c", d.eng)
                        if d.val > tgt.get(key, 0):
                            tgt[key] = d.val
                    for key in sorted(tgt, key=str):
                        v = tgt[key]
                        if hw.get(key, 0) >= v:
                            continue
                        hw[key] = v
                        sem = csems[key[1]] if key[0] == "c" else dsems[key[1]]
                        eng.wait_ge(sem, v)
                    if op.fn is None:
                        continue
                    inst = op.fn(eng)
                    if op.dma:
                        inst.then_inc(dsems[op.chan.sem_idx], 16)
                    elif op.marked:
                        inst.then_inc(csems[engname], 1)
            return emit_fn

        block.tensor(mk("tensor"))
        block.scalar(mk("scalar"))
        block.vector(mk("vector"))
        block.sync(mk("sync"))
        block.gpsimd(mk("gpsimd"))


def _host_constants():
    inv_freq = 10000.0 ** (-np.arange(0, DHR, 2, dtype=np.float64) / DHR)
    pos = np.arange(S, dtype=np.float64)
    ang = pos[None, :] * inv_freq[:, None]
    cos = np.cos(ang)
    sin = np.sin(ang)
    cosq = np.empty((128, S), dtype=np.float32)
    sinq = np.empty((128, S), dtype=np.float32)
    for r in range(128):
        i = (r % DHR) // 2
        cosq[r] = cos[i]
        sinq[r] = (-sin[i]) if (r % 2 == 0) else sin[i]
    perm = np.zeros((128, 128), dtype=np.float32)
    for i in range(128):
        perm[i, i ^ 1] = 1.0
    kk = np.arange(128)[:, None]
    qq = np.arange(128)[None, :]
    um = (kk <= qq).astype(np.float32)
    ub = 1.0 - um
    consts = np.zeros((1, 8), dtype=np.float32)
    consts[0, 0] = EPS
    consts[0, 1] = SCALE_MLA
    consts[0, 2] = SCALE_MHA
    for g in range(4):
        consts[0, 3 + g] = float(S - (g + 1) * QG)
    consts[0, 7] = -1.0
    return {
        "cosq": cosq,
        "sinq": sinq,
        "perm": perm,
        "umask": um,
        "ubar": ub,
        "consts": consts,
    }


def build_nc(upto=99, dbg=None):
    nc = bass.Bass("TRN2", target_bir_lowering=False,
                   detect_race_conditions=False)
    uid = itertools.count()

    def nm(s):
        return f"{s}_{next(uid)}"

    def inp(name, shape, dt=F32):
        return nc.dram_tensor(name, list(shape), dt, kind="ExternalInput")

    x_feat = inp("x_feat", [D, S])
    wdqT = inp("wdqT", [D, DL])
    wuqT = inp("wuqT", [DL, D])
    wqrT = inp("wqrT", [DL, H * DHR])
    wdkvT = inp("wdkvT", [D, DL])
    wukT = inp("wukT", [DL, D])
    wuvT = inp("wuvT", [DL, D])
    wkrT = inp("wkrT", [D, DHR])
    woT = inp("woT", [D, D])
    wq = inp("wq", [D, D])
    wk = inp("wk", [D, D])
    wv = inp("wv", [D, D])
    wo = inp("wo", [D, D])
    gidxT = inp("gidxT", [D, INTER], I32)
    uidxT = inp("uidxT", [D, INTER], I32)
    didxT = inp("didxT", [INTER, D], I32)
    lutrow = inp("lutrow", [1, 16])
    cosq_d = inp("cosq", [128, S])
    sinq_d = inp("sinq", [128, S])
    perm_d = inp("perm", [128, 128])
    um_d = inp("umask", [128, 128])
    ub_d = inp("ubar", [128, 128])
    consts_d = inp("consts", [1, 8])

    out_feat = nc.dram_tensor("out_feat", [D, S], F32, kind="ExternalOutput")

    qc_d = nc.dram_tensor("qc_d", [D, S], BF16)
    qr_d = nc.dram_tensor("qr_d", [H * DHR, S], BF16)
    kc_d = nc.dram_tensor("kc_d", [D, S], BF16)
    v_d = nc.dram_tensor("v_d", [S, D], BF16)
    ao_d = nc.dram_tensor("ao_d", [D, S], BF16)
    r1_d = nc.dram_tensor("r1_d", [D, S], F32)
    r2_d = nc.dram_tensor("r2_d", [D, S], F32)
    r3_d = nc.dram_tensor("r3_d", [D, S], F32)
    hp_d = nc.dram_tensor("hp_d", [INTER, S], BF16)
    wg_d = nc.dram_tensor("wg_d", [D, INTER], BF16)
    wu_d = nc.dram_tensor("wu_d", [D, INTER], BF16)
    wd_d = nc.dram_tensor("wd_d", [INTER, D], BF16)
    q2_d = nc.dram_tensor("q2_d", [D, S], BF16)
    k2_d = nc.dram_tensor("k2_d", [D, S], BF16)

    p = Plan()

    es = contextlib.ExitStack()
    with es:
        def sball(stk, name, shape, dt):
            return stk.enter_context(nc.sbuf_tensor(nm(name), list(shape), dt))

        def psall(stk, name, shape, dt=F32):
            return stk.enter_context(nc.psum_tensor(nm(name), list(shape), dt))

        consts_sb = sball(es, "consts", [128, 8], F32)
        b_consts = Buf("consts")
        onescol = sball(es, "onescol", [128, 1], BF16)
        b_onescol = Buf("onescol")
        onesrow = sball(es, "onesrow", [1, 128], BF16)
        b_onesrow = Buf("onesrow")
        onestile = sball(es, "onestile", [128, 512], BF16)
        b_onestile = Buf("onestile")
        umask = sball(es, "umask", [128, 128], BF16)
        b_umask = Buf("umask")
        ubar = sball(es, "ubar", [128, 128], BF16)
        b_ubar = Buf("ubar")
        perm = sball(es, "perm", [128, 128], BF16)
        b_perm = Buf("perm")
        cosq = sball(es, "cosq", [128, S], BF16)
        b_cosq = Buf("cosq")
        sinq = sball(es, "sinq", [128, S], BF16)
        b_sinq = Buf("sinq")
        krsb = sball(es, "krsb", [DHR, S], BF16)
        b_krsb = Buf("krsb")
        lutb = sball(es, "lutb", [128, 16], F32)
        b_lutb = Buf("lutb")

        C_EPS = consts_sb[0:1, 0:1]
        C_SMLA = consts_sb[:, 1:2]
        C_SMHA = consts_sb[:, 2:3]

        p.add("sync", lambda e: e.dma_start(
            consts_sb[:, :], bass.AP(consts_d, 0, [[0, 128], [1, 8]])),
            writes=[b_consts], dma=True)
        p.add("sync", lambda e: e.dma_start(
            lutb[:, :], bass.AP(lutrow, 0, [[0, 128], [1, 16]])),
            writes=[b_lutb], dma=True)
        p.add("gpsimd", lambda e: e.dma_start(cosq[:, :], cosq_d[:, :]),
              writes=[b_cosq], dma=True)
        p.add("gpsimd", lambda e: e.dma_start(sinq[:, :], sinq_d[:, :]),
              writes=[b_sinq], dma=True)
        p.add("gpsimd", lambda e: e.dma_start(perm[:, :], perm_d[:, :]),
              writes=[b_perm], dma=True)
        p.add("gpsimd", lambda e: e.dma_start(umask[:, :], um_d[:, :]),
              writes=[b_umask], dma=True)
        p.add("gpsimd", lambda e: e.dma_start(ubar[:, :], ub_d[:, :]),
              writes=[b_ubar], dma=True)
        p.add("vector", lambda e: e.memset(onescol[:, :], 1.0), writes=[b_onescol])
        p.add("vector", lambda e: e.memset(onesrow[:, :], 1.0), writes=[b_onesrow])
        p.add("vector", lambda e: e.memset(onestile[:, :], 1.0), writes=[b_onestile])

        wchunk_bufs = {}
        sel_jobs = ([(gidxT, wg_d, c, INTER) for c in range(NK)] +
                    [(uidxT, wu_d, c, INTER) for c in range(NK)])
        sel_jobs_d = [(didxT, wd_d, c, D) for c in range(KI)]
        if upto < 2:
            sel_jobs = []
            sel_jobs_d = []

        def make_sel_feeder(ph, jobs, width_max):
            six = [sball(ph, "sel_ix", [128, width_max], I32) for _ in range(2)]
            b_six = [Buf("sel_ix") for _ in range(2)]
            sixf = [sball(ph, "sel_ixf", [128, width_max], BF16)
                    for _ in range(2)]
            b_sixf = [Buf("sel_ixf") for _ in range(2)]
            sterm = sball(ph, "sel_term", [128, width_max], BF16)
            b_sterm = Buf("sel_term")
            sacc = [sball(ph, "sel_acc", [128, width_max], BF16)
                    for _ in range(2)]
            b_sacc = [Buf("sel_acc") for _ in range(2)]

            def _gen():
                for jn, (idx_dram, w_dram, c, width) in enumerate(jobs):
                    par = jn % 2
                    r0 = c * 128
                    p.add("sync", lambda e, s=idx_dram, r=r0, w=width,
                          t=six[par]: e.dma_start(t[:, 0:w], s[r:r + 128, :]),
                          writes=[b_six[par]], dma=True)
                    p.add("vector", lambda e, w=width, t=sixf[par],
                          s=six[par]: e.tensor_copy(t[:, 0:w], s[:, 0:w]),
                          reads=[b_six[par]], writes=[b_sixf[par]])
                    yield
                    p.add("vector", lambda e, w=width, t=sacc[par],
                          s=sixf[par]: e.tensor_scalar(
                              t[:, 0:w], s[:, 0:w], 0.0,
                              lutb[:, 0:1], ALU.is_equal, ALU.mult),
                          reads=[b_sixf[par], b_lutb], writes=[b_sacc[par]])
                    yield
                    for j in range(1, 16):
                        p.add("vector", lambda e, w=width, s=sixf[par], j=j:
                              e.tensor_scalar(
                                  sterm[:, 0:w], s[:, 0:w], float(j),
                                  lutb[:, j:j + 1], ALU.is_equal, ALU.mult),
                              reads=[b_sixf[par], b_lutb], writes=[b_sterm])
                        yield
                        p.add("vector", lambda e, w=width, t=sacc[par]:
                              e.tensor_tensor(t[:, 0:w], t[:, 0:w],
                                              sterm[:, 0:w], ALU.add),
                              reads=[b_sterm], writes=[b_sacc[par]])
                        yield
                    wb = Buf(f"wch_{w_dram.name}_{c}")
                    wchunk_bufs[(w_dram.name, c)] = wb
                    p.add("gpsimd", lambda e, d=w_dram, r=r0, w=width,
                          t=sacc[par]: e.dma_start(d[r:r + 128, :], t[:, 0:w]),
                          reads=[b_sacc[par]], writes=[wb], dma=True,
                          store=True)
            g = _gen()

            def step(n=1):
                for _ in range(n):
                    if next(g, "end") == "end":
                        break
            return step

        def ln_tmps(stk, tag):
            rb = [sball(stk, f"{tag}_rb", [128, 512], BF16) for _ in range(NK)]
            b_rb = [Buf(f"{tag}_rb{k}") for k in range(NK)]
            sqt = [sball(stk, f"{tag}_sq", [128, 512], BF16) for _ in range(2)]
            b_sqt = [Buf(f"{tag}_sq{i}") for i in range(2)]
            psm = psall(stk, f"{tag}_psm", [1, 512])
            psv = psall(stk, f"{tag}_psv", [1, 512])
            psb_ = psall(stk, f"{tag}_psb", [128, 1024])
            srow = sball(stk, f"{tag}_srow", [1, 2048], F32)
            bcb0 = sball(stk, f"{tag}_bcb0", [1, 512], BF16)
            bcb1 = sball(stk, f"{tag}_bcb1", [1, 512], BF16)
            bcb2 = sball(stk, f"{tag}_bcb2", [128, 1024], BF16)
            xtmp = [sball(stk, f"{tag}_xt", [128, 512], BF16) for _ in range(2)]
            b_xtmp = [Buf(f"{tag}_xt{i}") for i in range(2)]
            return dict(
                rb=rb, b_rb=b_rb, sqt=sqt, b_sqt=b_sqt,
                psm=psm, b_psm=Buf(f"{tag}_psm"),
                psv=psv, b_psv=Buf(f"{tag}_psv"),
                psb=psb_, b_psb=Buf(f"{tag}_psb"),
                srow=srow, b_srow=Buf(f"{tag}_srow"),
                bcb0=bcb0, b_bcb0=Buf(f"{tag}_b0"),
                bcb1=bcb1, b_bcb1=Buf(f"{tag}_b1"),
                bcb2=bcb2, b_bcb2=Buf(f"{tag}_b2"),
                xtmp=xtmp, b_xtmp=b_xtmp,
            )

        def layernorm_tg(src_dram, b_src, tg0, xn_tiles, b_xn, xn_col0, T):
            rb, b_rb = T["rb"], T["b_rb"]
            for k in range(NK):
                p.add("gpsimd", lambda e, k=k: e.dma_start(
                    rb[k][:, :], src_dram[k * 128:(k + 1) * 128, tg0:tg0 + 512]),
                    reads=[b_src], writes=[b_rb[k]], dma=True)
            for k in range(NK):
                p.add("tensor", lambda e, k=k: e.matmul(
                    T["psm"][0:1, :], onescol[:, :], rb[k][:, :],
                    start=(k == 0), stop=(k == NK - 1)),
                    reads=[b_rb[k], b_onescol], writes=[T["b_psm"]])
                par = k % 2
                p.add("scalar", lambda e, k=k, par=par: e.activation(
                    T["sqt"][par][:, :], rb[k][:, :], AF.Square),
                    reads=[b_rb[k]], writes=[T["b_sqt"][par]])
                p.add("tensor", lambda e, k=k, par=par: e.matmul(
                    T["psv"][0:1, :], onescol[:, :], T["sqt"][par][:, :],
                    start=(k == 0), stop=(k == NK - 1)),
                    reads=[T["b_sqt"][par], b_onescol], writes=[T["b_psv"]])
            sr = T["srow"]
            b_sr = T["b_srow"]
            p.add("vector", lambda e: e.tensor_scalar(
                sr[0:1, 0:512], T["psm"][0:1, :], 1.0 / D, None, ALU.mult),
                reads=[T["b_psm"]], writes=[b_sr])
            p.add("vector", lambda e: e.tensor_scalar(
                sr[0:1, 512:1024], T["psv"][0:1, :], 1.0 / D, None, ALU.mult),
                reads=[T["b_psv"]], writes=[b_sr])
            p.add("vector", lambda e: e.tensor_tensor(
                sr[0:1, 1024:1536], sr[0:1, 0:512], sr[0:1, 0:512], ALU.mult),
                writes=[b_sr])
            p.add("vector", lambda e: e.tensor_tensor(
                sr[0:1, 512:1024], sr[0:1, 512:1024], sr[0:1, 1024:1536],
                ALU.subtract), writes=[b_sr])
            p.add("scalar", lambda e: e.activation(
                sr[0:1, 1024:1536], sr[0:1, 512:1024], AF.Sqrt, bias=C_EPS),
                reads=[b_consts], writes=[b_sr])
            p.add("vector", lambda e: e.reciprocal(
                sr[0:1, 1536:2048], sr[0:1, 1024:1536]), writes=[b_sr])
            p.add("vector", lambda e: e.tensor_copy(
                T["bcb0"][0:1, :], sr[0:1, 0:512]),
                reads=[b_sr], writes=[T["b_bcb0"]])
            p.add("vector", lambda e: e.tensor_copy(
                T["bcb1"][0:1, :], sr[0:1, 1536:2048]),
                reads=[b_sr], writes=[T["b_bcb1"]])
            p.add("tensor", lambda e: e.matmul(
                T["psb"][:, 0:512], onesrow[:, :], T["bcb0"][0:1, :],
                start=True, stop=True),
                reads=[T["b_bcb0"], b_onesrow], writes=[T["b_psb"]])
            p.add("tensor", lambda e: e.matmul(
                T["psb"][:, 512:1024], onesrow[:, :], T["bcb1"][0:1, :],
                start=True, stop=True),
                reads=[T["b_bcb1"], b_onesrow], writes=[T["b_psb"]])
            p.add("scalar", lambda e: e.activation(
                T["bcb2"][:, :], T["psb"][:, :], AF.Copy),
                reads=[T["b_psb"]], writes=[T["b_bcb2"]])
            for k in range(NK):
                par = k % 2
                p.add("vector", lambda e, k=k, par=par: e.tensor_tensor(
                    T["xtmp"][par][:, :], rb[k][:, :], T["bcb2"][:, 0:512],
                    ALU.subtract),
                    reads=[b_rb[k], T["b_bcb2"]], writes=[T["b_xtmp"][par]])
                p.add("vector", lambda e, k=k, par=par: e.tensor_tensor(
                    xn_tiles[k][:, xn_col0:xn_col0 + 512], T["xtmp"][par][:, :],
                    T["bcb2"][:, 512:1024], ALU.mult),
                    reads=[T["b_xtmp"][par], T["b_bcb2"]], writes=[b_xn[k]])

        b_qc = Buf("qc_dram")
        b_qr = Buf("qr_dram")
        b_kc = Buf("kc_dram")
        b_v = Buf("v_dram")
        with contextlib.ExitStack() as phA:
            cq = [sball(phA, "a_cq", [128, S], BF16) for _ in range(NL)]
            b_cq = [Buf(f"a_cq{k}") for k in range(NL)]
            ckv = [sball(phA, "a_ckv", [128, S], BF16) for _ in range(NL)]
            b_ckv = [Buf(f"a_ckv{k}") for k in range(NL)]

            with contextlib.ExitStack() as ph:
                wdq = [sball(ph, "a_wdq", [128, DL], BF16) for _ in range(NK)]
                b_wdq = [Buf(f"a_wdq{k}") for k in range(NK)]
                wdkv = [sball(ph, "a_wdkv", [128, DL], BF16) for _ in range(NK)]
                b_wdkv = [Buf(f"a_wdkv{k}") for k in range(NK)]
                wkr = [sball(ph, "a_wkr", [128, DHR], BF16) for _ in range(NK)]
                b_wkr = [Buf(f"a_wkr{k}") for k in range(NK)]
                for k in range(NK):
                    p.add("gpsimd", lambda e, k=k: e.dma_start(
                        wdq[k][:, :], wdqT[k * 128:(k + 1) * 128, :]),
                        writes=[b_wdq[k]], dma=True)
                    p.add("gpsimd", lambda e, k=k: e.dma_start(
                        wdkv[k][:, :], wdkvT[k * 128:(k + 1) * 128, :]),
                        writes=[b_wdkv[k]], dma=True)
                    p.add("gpsimd", lambda e, k=k: e.dma_start(
                        wkr[k][:, :], wkrT[k * 128:(k + 1) * 128, :]),
                        writes=[b_wkr[k]], dma=True)

                xn = [[sball(ph, "a_xn", [128, 512], BF16) for _ in range(NK)]
                      for _ in range(2)]
                b_xn = [[Buf(f"a_xn{i}_{k}") for k in range(NK)] for i in range(2)]
                T = ln_tmps(ph, "aln")
                psp = [psall(ph, "a_psp", [128, 512]) for _ in range(2)]
                b_psp = [Buf(f"a_psp{i}") for i in range(2)]
                pskr = psall(ph, "a_pskr", [64, 512])
                b_pskr = Buf("a_pskr")
                pswp = psall(ph, "a_pswp", [64, 512])
                b_pswp = Buf("a_pswp")
                krr = [sball(ph, "a_krr", [64, 512], BF16) for _ in range(4)]
                b_krr = [Buf(f"a_krr{i}") for i in range(4)]
                b_xin = Buf("x_input")

                for tg in range(8 if upto >= 3 else 0):
                    ring = tg % 2
                    layernorm_tg(x_feat, b_xin, tg * 512, xn[ring], b_xn[ring], 0, T)
                    for which, wt, b_wt, dst, b_dst in (
                            (0, wdq, b_wdq, cq, b_cq),
                            (1, wdkv, b_wdkv, ckv, b_ckv)):
                        for do in range(NL):
                            i = (which * NL + do) % 2
                            ps, bps = psp[i], b_psp[i]

                            def mmgrp(e, wt=wt, do=do, ring=ring, ps=ps):
                                last = None
                                for k in range(NK):
                                    last = e.matmul(
                                        ps[:, :],
                                        wt[k][:, do * 128:(do + 1) * 128],
                                        xn[ring][k][:, :],
                                        start=(k == 0), stop=(k == NK - 1))
                                return last
                            p.add("tensor", mmgrp,
                                  reads=[b_wt[k] for k in range(NK)] + b_xn[ring],
                                  writes=[bps])
                            p.add("scalar",
                                  lambda e, dst=dst, do=do, tg=tg, ps=ps:
                                  e.activation(
                                      dst[do][:, tg * 512:(tg + 1) * 512],
                                      ps[:, :], AF.Copy),
                                  reads=[bps], writes=[b_dst[do]])

                    def krgrp(e, ring=ring):
                        last = None
                        for k in range(NK):
                            last = e.matmul(pskr[:, :], wkr[k][:, :],
                                            xn[ring][k][:, :],
                                            start=(k == 0), stop=(k == NK - 1))
                        return last
                    p.add("tensor", krgrp,
                          reads=[b_wkr[k] for k in range(NK)] + b_xn[ring],
                          writes=[b_pskr])
                    ia = 2 * (tg % 2)
                    kr_raw, b_raw = krr[ia], b_krr[ia]
                    kr_t1, b_t1 = krr[ia + 1], b_krr[ia + 1]
                    p.add("scalar", lambda e, t=kr_raw: e.activation(
                        t[:, :], pskr[:, :], AF.Copy),
                        reads=[b_pskr], writes=[b_raw])
                    p.add("tensor", lambda e, t=kr_raw: e.matmul(
                        pswp[:, :], perm[0:64, 0:64], t[:, :],
                        start=True, stop=True),
                        reads=[b_raw, b_perm], writes=[b_pswp])
                    p.add("vector", lambda e, t=kr_raw, o=kr_t1, tg=tg:
                          e.tensor_tensor(o[:, :], t[:, :],
                                          cosq[0:64, tg * 512:(tg + 1) * 512],
                                          ALU.mult),
                          reads=[b_raw, b_cosq], writes=[b_t1])
                    p.add("vector", lambda e, t=kr_raw, tg=tg: e.tensor_tensor(
                        t[:, :], pswp[:, :],
                        sinq[0:64, tg * 512:(tg + 1) * 512], ALU.mult),
                        reads=[b_pswp, b_sinq], writes=[b_raw])
                    p.add("vector", lambda e, t=kr_raw, o=kr_t1, tg=tg:
                          e.tensor_tensor(krsb[:, tg * 512:(tg + 1) * 512],
                                          o[:, :], t[:, :], ALU.add),
                          reads=[b_raw, b_t1], writes=[b_krsb])
                p.barrier()

            with contextlib.ExitStack() as ph:
                wuq = [sball(ph, "a2_wuq", [128, D], BF16) for _ in range(NL)]
                b_wuq = [Buf(f"a2_wuq{k}") for k in range(NL)]
                wqr = [sball(ph, "a2_wqr", [128, H * DHR], BF16) for _ in range(NL)]
                b_wqr = [Buf(f"a2_wqr{k}") for k in range(NL)]
                wuk = [sball(ph, "a2_wuk", [128, D], BF16) for _ in range(NL)]
                b_wuk = [Buf(f"a2_wuk{k}") for k in range(NL)]
                wuv = [sball(ph, "a2_wuv", [128, D], BF16) for _ in range(NL)]
                b_wuv = [Buf(f"a2_wuv{k}") for k in range(NL)]
                for k in range(NL if upto >= 4 else 0):
                    for wt, b_wt, srcd in ((wuq, b_wuq, wuqT), (wqr, b_wqr, wqrT),
                                           (wuk, b_wuk, wukT), (wuv, b_wuv, wuvT)):
                        p.add("gpsimd", lambda e, k=k, wt=wt, srcd=srcd:
                              e.dma_start(wt[k][:, :],
                                          srcd[k * 128:(k + 1) * 128, :]),
                              writes=[b_wt[k]], dma=True)

                psq = [psall(ph, "a2_ps", [128, 512]) for _ in range(4)]
                b_psq = [Buf(f"a2_ps{i}") for i in range(4)]
                pswq = psall(ph, "a2_pswq", [128, 512])
                b_pswq = Buf("a2_pswq")
                ev = [sball(ph, "a2_ev", [128, 512], BF16) for _ in range(6)]
                b_ev = [Buf(f"a2_ev{i}") for i in range(6)]

                un = [0]
                for wt, b_wt, src, b_src_l, dst_d, b_dst in (() if upto < 4 else (
                        (wuq, b_wuq, cq, b_cq, qc_d, b_qc),
                        (wuk, b_wuk, ckv, b_ckv, kc_d, b_kc))):
                    for do in range(NK):
                        for sub in range(8):
                            i = un[0] % 4
                            ps, bps = psq[i], b_psq[i]
                            et, bet = ev[un[0] % 6], b_ev[un[0] % 6]
                            un[0] += 1

                            def mmg(e, wt=wt, src=src, do=do, sub=sub, ps=ps):
                                last = None
                                for k in range(NL):
                                    last = e.matmul(
                                        ps[:, :],
                                        wt[k][:, do * 128:(do + 1) * 128],
                                        src[k][:, sub * 512:(sub + 1) * 512],
                                        start=(k == 0), stop=(k == NL - 1))
                                return last
                            p.add("tensor", mmg,
                                  reads=[b_wt[k] for k in range(NL)] + b_src_l,
                                  writes=[bps])
                            p.add("scalar", lambda e, et=et, ps=ps: e.activation(
                                et[:, :], ps[:, :], AF.Copy),
                                reads=[bps], writes=[bet])
                            p.add("gpsimd", lambda e, et=et, dst_d=dst_d, do=do,
                                  sub=sub: e.dma_start(
                                      dst_d[do * 128:(do + 1) * 128,
                                            sub * 512:(sub + 1) * 512], et[:, :]),
                                  reads=[bet], writes=[b_dst], dma=True,
                                  store=True)

                for do in range(H * DHR // 128 if upto >= 4 else 0):
                    for sub in range(8):
                        i = un[0] % 4
                        ps, bps = psq[i], b_psq[i]
                        raw, braw = ev[un[0] % 6], b_ev[un[0] % 6]
                        t1, bt1 = ev[(un[0] + 1) % 6], b_ev[(un[0] + 1) % 6]
                        un[0] += 2

                        def mmg(e, do=do, sub=sub, ps=ps):
                            last = None
                            for k in range(NL):
                                last = e.matmul(
                                    ps[:, :], wqr[k][:, do * 128:(do + 1) * 128],
                                    cq[k][:, sub * 512:(sub + 1) * 512],
                                    start=(k == 0), stop=(k == NL - 1))
                            return last
                        p.add("tensor", mmg,
                              reads=[b_wqr[k] for k in range(NL)] + b_cq,
                              writes=[bps])
                        p.add("scalar", lambda e, raw=raw, ps=ps: e.activation(
                            raw[:, :], ps[:, :], AF.Copy), reads=[bps],
                            writes=[braw])
                        p.add("tensor", lambda e, raw=raw: e.matmul(
                            pswq[:, :], perm[:, :], raw[:, :],
                            start=True, stop=True),
                            reads=[braw, b_perm], writes=[b_pswq])
                        p.add("vector", lambda e, raw=raw, t1=t1, sub=sub:
                              e.tensor_tensor(
                                  t1[:, :], raw[:, :],
                                  cosq[:, sub * 512:(sub + 1) * 512], ALU.mult),
                              reads=[braw, b_cosq], writes=[bt1])
                        p.add("vector", lambda e, raw=raw, sub=sub:
                              e.tensor_tensor(
                                  raw[:, :], pswq[:, :],
                                  sinq[:, sub * 512:(sub + 1) * 512], ALU.mult),
                              reads=[b_pswq, b_sinq], writes=[braw])
                        p.add("vector", lambda e, raw=raw, t1=t1: e.tensor_tensor(
                            t1[:, :], t1[:, :], raw[:, :], ALU.add),
                            reads=[braw], writes=[bt1])
                        p.add("gpsimd", lambda e, t1=t1, do=do, sub=sub:
                              e.dma_start(
                                  qr_d[do * 128:(do + 1) * 128,
                                       sub * 512:(sub + 1) * 512], t1[:, :]),
                              reads=[bt1], writes=[b_qr], dma=True, store=True)

                for tc in range(NT if upto >= 4 else 0):
                    for no in range(4):
                        i = un[0] % 4
                        ps, bps = psq[i], b_psq[i]
                        et, bet = ev[un[0] % 6], b_ev[un[0] % 6]
                        un[0] += 1

                        def mmg(e, tc=tc, no=no, ps=ps):
                            last = None
                            for k in range(NL):
                                last = e.matmul(
                                    ps[:, :], ckv[k][:, tc * 128:(tc + 1) * 128],
                                    wuv[k][:, no * 512:(no + 1) * 512],
                                    start=(k == 0), stop=(k == NL - 1))
                            return last
                        p.add("tensor", mmg,
                              reads=[b_wuv[k] for k in range(NL)] + b_ckv,
                              writes=[bps])
                        p.add("scalar", lambda e, et=et, ps=ps: e.activation(
                            et[:, :], ps[:, :], AF.Copy), reads=[bps],
                            writes=[bet])
                        p.add("gpsimd", lambda e, et=et, tc=tc, no=no:
                              e.dma_start(
                                  v_d[tc * 128:(tc + 1) * 128,
                                      no * 512:(no + 1) * 512], et[:, :]),
                              reads=[bet], writes=[b_v], dma=True, store=True)
                p.barrier()

        def attention(mla, qd, b_qd, qrd, b_qrd, kd, b_kd, vd, b_vd, aod,
                      b_aod, sel=None):
            with contextlib.ExitStack() as ph:
                ksb = [sball(ph, "at_k", [128, S], BF16) for _ in range(2)]
                b_ksb = [Buf(f"at_k{i}") for i in range(2)]
                vsb = [sball(ph, "at_v", [128, S], BF16) for _ in range(2)]
                b_vsb = [Buf(f"at_v{i}") for i in range(2)]
                qcs = [sball(ph, "at_qc", [128, QG], BF16) for _ in range(2)]
                b_qcs = [Buf(f"at_qc{i}") for i in range(2)]
                qrs = [sball(ph, "at_qr", [64, QG], BF16) for _ in range(2)]
                b_qrs = [Buf(f"at_qr{i}") for i in range(2)]
                psS = [psall(ph, "at_S", [128, QG]) for _ in range(2)]
                b_psS = [Buf(f"at_S{i}") for i in range(2)]
                psO = psall(ph, "at_O", [128, QG])
                b_psO = Buf("at_O")
                psA = psall(ph, "at_A", [128, QG])
                b_psA = Buf("at_A")
                pt = [sball(ph, "at_p", [128, QG], BF16) for _ in range(2)]
                b_pt = [Buf(f"at_p{i}") for i in range(2)]
                rr = sball(ph, "at_rr", [1, 2 * QG], F32)
                b_rr = Buf("at_rr")
                rrb = sball(ph, "at_rrb", [1, QG], BF16)
                b_rrb = Buf("at_rrb")
                bcb = [sball(ph, "at_bc", [128, QG], BF16) for _ in range(2)]
                b_bcb = [Buf(f"at_bc{i}") for i in range(2)]
                ob = [sball(ph, "at_ob", [128, QG], BF16) for _ in range(2)]
                b_ob = [Buf(f"at_ob{i}") for i in range(2)]
                suf = sball(ph, "at_suf", [128, 3], F32)
                b_suf = Buf("at_suf")

                if sel:
                    sel_step = make_sel_feeder(ph, sel, INTER)
                else:
                    def sel_step(n=1):
                        pass

                scale_ap = C_SMLA if mla else C_SMHA

                for h in range(H):
                    hp = h % 2
                    p.add("sync", lambda e, h=h, t=ksb[hp]: e.dma_start(
                        t[:, :], kd[h * 128:(h + 1) * 128, :]),
                        reads=[b_kd], writes=[b_ksb[hp]], dma=True)
                    for g8 in range(4):
                        p.add("sync", lambda e, h=h, g8=g8, t=vsb[hp]:
                              e.dma_start(
                                  bass.AP(t, g8 * 1024,
                                          [[S, 128], [128, 8], [1, 128]]),
                                  bass.AP(vd, g8 * 8 * 128 * D + h * 128,
                                          [[D, 128], [128 * D, 8], [1, 128]])),
                              reads=[b_vd], writes=[b_vsb[hp]], dma=True)
                    if not mla:
                        def sufmm(e, hp=hp, lo=None, hi=None, first=False,
                                  last=False):
                            pass
                        for gi, (lo, hi) in enumerate(((24, 32), (16, 24),
                                                       (8, 16))):
                            def fn(e, hp=hp, lo=lo, hi=hi, first=(gi == 0),
                                   last=(gi == 2)):
                                lastm = None
                                for kc in range(hi - 1, lo - 1, -1):
                                    lastm = e.matmul(
                                        psA[:, 0:1],
                                        vsb[hp][:, kc * 128:(kc + 1) * 128],
                                        onescol[:, :],
                                        start=(first and kc == hi - 1),
                                        stop=(last and kc == lo))
                                return lastm
                            p.add("tensor", fn,
                                  reads=[b_vsb[hp], b_onescol],
                                  writes=[b_psA])
                            p.add("scalar", lambda e, gi=gi: e.activation(
                                suf[:, 2 - gi:3 - gi], psA[:, 0:1], AF.Copy),
                                reads=[b_psA], writes=[b_suf])
                    for g in range(NQG):
                        gp = g % 2
                        q0 = g * QG
                        p.add("sync", lambda e, h=h, q0=q0, t=qcs[gp]:
                              e.dma_start(t[:, :],
                                          qd[h * 128:(h + 1) * 128, q0:q0 + QG]),
                              reads=[b_qd], writes=[b_qcs[gp]], dma=True)
                        if mla:
                            p.add("sync", lambda e, h=h, q0=q0, t=qrs[gp]:
                                  e.dma_start(
                                      t[:, :],
                                      qrd[h * DHR:(h + 1) * DHR, q0:q0 + QG]),
                                  reads=[b_qrd], writes=[b_qrs[gp]], dma=True)
                        kc_hi = (g + 1) * 8
                        nkc_total = kc_hi

                        def dp(kc):
                            par = kc % 2

                            def fn(e, kc=kc, par=par, hp=hp,
                                   start=(kc == 0), stop=(kc == nkc_total - 1)):
                                last = None
                                for half in range(2):
                                    c0 = half * 512
                                    rhs = pt[par][:, c0:c0 + 512]
                                    e.matmul(psA[0:1, c0:c0 + 512],
                                             onescol[:, :], rhs,
                                             start=start, stop=stop)
                                    last = e.matmul(
                                        psO[:, c0:c0 + 512],
                                        vsb[hp][:, kc * 128:(kc + 1) * 128],
                                        rhs, start=start, stop=stop)
                                return last
                            p.add("tensor", fn,
                                  reads=[b_pt[par], b_vsb[hp], b_onescol],
                                  writes=[b_psO, b_psA])

                        pending = None
                        for kc in range(kc_hi):
                            par = kc % 2

                            def smm(e, kc=kc, par=par, gp=gp, hp=hp):
                                last = None
                                for half in range(2):
                                    c0 = half * 512
                                    if mla:
                                        e.matmul(psS[par][:, c0:c0 + 512],
                                                 ksb[hp][:, kc * 128:(kc + 1) * 128],
                                                 qcs[gp][:, c0:c0 + 512],
                                                 start=True, stop=False)
                                        last = e.matmul(
                                            psS[par][:, c0:c0 + 512],
                                            krsb[:, kc * 128:(kc + 1) * 128],
                                            qrs[gp][:, c0:c0 + 512],
                                            start=False, stop=True)
                                    else:
                                        last = e.matmul(
                                            psS[par][:, c0:c0 + 512],
                                            ksb[hp][:, kc * 128:(kc + 1) * 128],
                                            qcs[gp][:, c0:c0 + 512],
                                            start=True, stop=True)
                                return last
                            reads = [b_ksb[hp], b_qcs[gp]]
                            if mla:
                                reads += [b_krsb, b_qrs[gp]]
                            p.add("tensor", smm, reads=reads, writes=[b_psS[par]])
                            p.add("scalar", lambda e, par=par: e.activation(
                                pt[par][:, :], psS[par][:, :], AF.Exp,
                                scale=scale_ap),
                                reads=[b_psS[par], b_consts], writes=[b_pt[par]])
                            if mla:
                                p.add("vector", lambda e, par=par:
                                      e.tensor_scalar(
                                          pt[par][:, :], pt[par][:, :],
                                          CLAMP_HI, CLAMP_LO,
                                          ALU.min, ALU.max),
                                      writes=[b_pt[par]])
                            if kc * 128 >= q0:
                                w = kc * 128 - q0
                                if w > 0:
                                    fv = 0.0 if mla else 1.0
                                    p.add("vector", lambda e, par=par, w=w,
                                          fv=fv: e.memset(pt[par][:, 0:w], fv),
                                          writes=[b_pt[par]])
                                p.add("vector", lambda e, par=par, w=w:
                                      e.tensor_tensor(pt[par][:, w:w + 128],
                                                      pt[par][:, w:w + 128],
                                                      umask[:, :], ALU.mult),
                                      reads=[b_umask], writes=[b_pt[par]])
                                if not mla:
                                    p.add("vector", lambda e, par=par, w=w:
                                          e.tensor_tensor(pt[par][:, w:w + 128],
                                                          pt[par][:, w:w + 128],
                                                          ubar[:, :], ALU.add),
                                          reads=[b_ubar], writes=[b_pt[par]])
                            if pending is not None:
                                dp(pending)
                            pending = kc
                            sel_step(2)
                        if pending is not None:
                            dp(pending)
                        if mla:
                            p.add("scalar", lambda e: e.activation(
                                rr[0:1, 0:QG], psA[0:1, 0:QG], AF.Ln),
                                reads=[b_psA, b_consts], writes=[b_rr])
                        else:
                            p.add("scalar", lambda e, g=g: e.activation(
                                rr[0:1, 0:QG], psA[0:1, 0:QG], AF.Ln,
                                bias=consts_sb[0:1, 3 + g:4 + g]),
                                reads=[b_psA, b_consts], writes=[b_rr])
                        p.add("scalar", lambda e: e.activation(
                            rrb[0:1, :], rr[0:1, 0:QG], AF.Exp,
                            scale=consts_sb[0:1, 7:8]),
                            reads=[b_rr, b_consts], writes=[b_rrb])

                        def bcmm(e):
                            last = None
                            for half in range(2):
                                c0 = half * 512
                                last = e.matmul(psA[:, c0:c0 + 512],
                                                onesrow[:, :],
                                                rrb[0:1, c0:c0 + 512],
                                                start=True, stop=True)
                            return last
                        p.add("tensor", bcmm, reads=[b_rrb, b_onesrow],
                              writes=[b_psA])
                        p.add("scalar", lambda e, gp=gp: e.activation(
                            bcb[gp][:, :], psA[:, :], AF.Copy),
                            reads=[b_psA], writes=[b_bcb[gp]])
                        if mla or g == NQG - 1:
                            p.add("vector", lambda e, gp=gp: e.tensor_tensor(
                                ob[gp][:, :], psO[:, :], bcb[gp][:, :],
                                ALU.mult),
                                reads=[b_psO, b_bcb[gp]], writes=[b_ob[gp]])
                        else:
                            p.add("vector", lambda e, gp=gp, g=g:
                                  e.scalar_tensor_tensor(
                                      ob[gp][:, :], psO[:, :],
                                      suf[:, g:g + 1], bcb[gp][:, :],
                                      ALU.add, ALU.mult),
                                  reads=[b_psO, b_bcb[gp], b_suf],
                                  writes=[b_ob[gp]])
                        p.add("gpsimd", lambda e, h=h, q0=q0, gp=gp:
                              e.dma_start(
                                  aod[h * 128:(h + 1) * 128, q0:q0 + QG],
                                  ob[gp][:, :]),
                              reads=[b_ob[gp]], writes=[b_aod], dma=True,
                              store=True)
                sel_step(1 << 30)
                p.barrier()

        def out_proj(w_dram, aod, b_aod, resid_d, b_resid, out_d, b_out):
            with contextlib.ExitStack() as ph:
                wsb = [sball(ph, "op_w", [128, D], BF16) for _ in range(NK)]
                b_wsb = [Buf(f"op_w{k}") for k in range(NK)]
                aos = [sball(ph, "op_ao", [128, 2048], BF16) for _ in range(NK)]
                b_aos = [Buf(f"op_ao{k}") for k in range(NK)]
                ps = [psall(ph, "op_ps", [128, 512]) for _ in range(4)]
                b_ps = [Buf(f"op_ps{i}") for i in range(4)]
                rt = [sball(ph, "op_r", [128, 512], F32) for _ in range(4)]
                b_rt = [Buf(f"op_r{i}") for i in range(4)]
                rn = [sball(ph, "op_rn", [128, 512], F32) for _ in range(4)]
                b_rn = [Buf(f"op_rn{i}") for i in range(4)]

                for k in range(NK):
                    p.add("gpsimd", lambda e, k=k: e.dma_start(
                        wsb[k][:, :], w_dram[k * 128:(k + 1) * 128, :]),
                        writes=[b_wsb[k]], dma=True)
                un = 0
                for tb in range(2):
                    c0 = tb * 2048
                    for k in range(NK):
                        p.add("sync", lambda e, k=k, c0=c0: e.dma_start(
                            aos[k][:, :],
                            aod[k * 128:(k + 1) * 128, c0:c0 + 2048]),
                            reads=[b_aod], writes=[b_aos[k]], dma=True)
                    for do in range(NK):
                        for sub in range(4):
                            i = un % 4
                            un += 1
                            cc = c0 + sub * 512

                            def mmg(e, do=do, sub=sub, i=i):
                                last = None
                                for k in range(NK):
                                    last = e.matmul(
                                        ps[i][:, :],
                                        wsb[k][:, do * 128:(do + 1) * 128],
                                        aos[k][:, sub * 512:(sub + 1) * 512],
                                        start=(k == 0), stop=(k == NK - 1))
                                return last
                            p.add("tensor", mmg,
                                  reads=[b_wsb[k] for k in range(NK)] +
                                        [b_aos[k] for k in range(NK)],
                                  writes=[b_ps[i]])
                            p.add("sync", lambda e, do=do, cc=cc, i=i:
                                  e.dma_start(
                                      rt[i][:, :],
                                      resid_d[do * 128:(do + 1) * 128,
                                              cc:cc + 512]),
                                  reads=[b_resid], writes=[b_rt[i]], dma=True)
                            p.add("vector", lambda e, i=i: e.tensor_tensor(
                                rn[i][:, :], ps[i][:, :], rt[i][:, :], ALU.add),
                                reads=[b_ps[i], b_rt[i]], writes=[b_rn[i]])
                            p.add("gpsimd", lambda e, do=do, cc=cc, i=i:
                                  e.dma_start(
                                      out_d[do * 128:(do + 1) * 128, cc:cc + 512],
                                      rn[i][:, :]),
                                  reads=[b_rn[i]], writes=[b_out], dma=True,
                                  store=True)
                p.barrier()

        def ffn(resid_d, b_resid, out_d, b_out, sel=None):
            b_hp = [[Buf(f"hp_{c}_{t}") for t in range(8)] for c in range(KI)]
            with contextlib.ExitStack() as ph:
                xnb = [[sball(ph, "f_xn", [128, 1024], BF16)
                        for _ in range(NK)] for _ in range(2)]
                b_xnb = [[Buf(f"f_xn{r}_{k}") for k in range(NK)]
                         for r in range(2)]
                T = ln_tmps(ph, "fln")
                wgt = [sball(ph, "f_wg", [128, 2048], BF16) for _ in range(2)]
                b_wgt = [Buf(f"f_wg{i}") for i in range(2)]
                wut = [sball(ph, "f_wu", [128, 2048], BF16) for _ in range(2)]
                b_wut = [Buf(f"f_wu{i}") for i in range(2)]
                psg = [psall(ph, "f_pg", [128, 512]) for _ in range(2)]
                b_psg = [Buf(f"f_pg{i}") for i in range(2)]
                psu = [psall(ph, "f_pu", [128, 512]) for _ in range(2)]
                b_psu = [Buf(f"f_pu{i}") for i in range(2)]
                sg = [sball(ph, "f_sg", [128, 512], BF16) for _ in range(2)]
                b_sg = [Buf(f"f_sg{i}") for i in range(2)]
                hpt = [sball(ph, "f_hp", [128, 512], BF16) for _ in range(2)]
                b_hpt = [Buf(f"f_hp{i}") for i in range(2)]
                if sel:
                    fsel_step = make_sel_feeder(ph, sel, D)
                else:
                    def fsel_step(n=1):
                        pass

                for tb in range(4):
                    ring = tb % 2
                    c0 = tb * 1024
                    for tg in range(2):
                        layernorm_tg(resid_d, b_resid, c0 + tg * 512,
                                     xnb[ring], b_xnb[ring], tg * 512, T)
                    for do in range(KI):
                        dpar = do % 2
                        p.add("sync", lambda e, do=do, t=wgt[dpar]: e.dma_start(
                            bass.AP(t, 0, [[2048, 128], [128, NK], [1, 128]]),
                            bass.AP(wg_d, do * 128,
                                    [[INTER, 128], [128 * INTER, NK], [1, 128]])),
                            reads=[wchunk_bufs[("wg_d", k)] for k in range(NK)],
                            writes=[b_wgt[dpar]], dma=True)
                        p.add("sync", lambda e, do=do, t=wut[dpar]: e.dma_start(
                            bass.AP(t, 0, [[2048, 128], [128, NK], [1, 128]]),
                            bass.AP(wu_d, do * 128,
                                    [[INTER, 128], [128 * INTER, NK], [1, 128]])),
                            reads=[wchunk_bufs[("wu_d", k)] for k in range(NK)],
                            writes=[b_wut[dpar]], dma=True)
                        for sub in range(2):
                            spar = sub % 2

                            def gmm(e, dpar=dpar, spar=spar, sub=sub,
                                    ring=ring):
                                last = None
                                for k in range(NK):
                                    last = e.matmul(
                                        psg[spar][:, :],
                                        wgt[dpar][:, k * 128:(k + 1) * 128],
                                        xnb[ring][k][:, sub * 512:(sub + 1) * 512],
                                        start=(k == 0), stop=(k == NK - 1))
                                return last
                            p.add("tensor", gmm,
                                  reads=[b_wgt[dpar]] + b_xnb[ring],
                                  writes=[b_psg[spar]])

                            def umm(e, dpar=dpar, spar=spar, sub=sub,
                                    ring=ring):
                                last = None
                                for k in range(NK):
                                    last = e.matmul(
                                        psu[spar][:, :],
                                        wut[dpar][:, k * 128:(k + 1) * 128],
                                        xnb[ring][k][:, sub * 512:(sub + 1) * 512],
                                        start=(k == 0), stop=(k == NK - 1))
                                return last
                            p.add("tensor", umm,
                                  reads=[b_wut[dpar]] + b_xnb[ring],
                                  writes=[b_psu[spar]])
                            p.add("scalar", lambda e, spar=spar: e.activation(
                                sg[spar][:, :], psg[spar][:, :], AF.Silu),
                                reads=[b_psg[spar]], writes=[b_sg[spar]])
                            p.add("vector", lambda e, spar=spar: e.tensor_tensor(
                                hpt[spar][:, :], sg[spar][:, :],
                                psu[spar][:, :], ALU.mult),
                                reads=[b_sg[spar], b_psu[spar]],
                                writes=[b_hpt[spar]])
                            tcol = tb * 2 + sub
                            p.add("gpsimd", lambda e, do=do, c0=c0, sub=sub,
                                  spar=spar: e.dma_start(
                                      hp_d[do * 128:(do + 1) * 128,
                                           c0 + sub * 512:c0 + (sub + 1) * 512],
                                      hpt[spar][:, :]),
                                  reads=[b_hpt[spar]],
                                  writes=[b_hp[do][tcol]], dma=True, store=True)
                            fsel_step(5)
                fsel_step(1 << 30)
                p.barrier()
            with contextlib.ExitStack() as ph:
                hps = [sball(ph, "d_hp", [128, KI * 512], BF16)
                       for _ in range(2)]
                b_hps = [Buf(f"d_hp{h}") for h in range(2)]
                wdt = [sball(ph, "d_wd", [128, INTER], BF16) for _ in range(2)]
                b_wdt = [Buf(f"d_wd{i}") for i in range(2)]
                psd = [psall(ph, "d_ps", [128, 512]) for _ in range(4)]
                b_psd = [Buf(f"d_ps{i}") for i in range(4)]
                rt = [sball(ph, "d_r", [128, 512], F32) for _ in range(4)]
                b_rt = [Buf(f"d_r{i}") for i in range(4)]
                rn = [sball(ph, "d_rn", [128, 512], F32) for _ in range(4)]
                b_rn = [Buf(f"d_rn{i}") for i in range(4)]

                un = 0
                for td in range(4):
                    c0 = td * 1024
                    for half in range(2):
                        tc = c0 // 512 + half
                        for qq in range(4):
                            kl = (KI * qq) // 4
                            kh = (KI * (qq + 1)) // 4
                            p.add("sync", lambda e, tc=tc, half=half, kl=kl,
                                  kh=kh: e.dma_start(
                                      bass.AP(hps[half], kl * 512,
                                              [[KI * 512, 128],
                                               [512, kh - kl], [1, 512]]),
                                      bass.AP(hp_d, kl * 128 * S + tc * 512,
                                              [[S, 128], [128 * S, kh - kl],
                                               [1, 512]])),
                                  reads=[b_hp[k][tc] for k in range(kl, kh)],
                                  writes=[b_hps[half]], dma=True)
                    for do in range(NK):
                        dpar = do % 2
                        for q in range(4):
                            klo = (KI * q) // 4
                            khi = (KI * (q + 1)) // 4
                            p.add("sync", lambda e, do=do, kl=klo, kh=khi,
                                  t=wdt[dpar]: e.dma_start(
                                      bass.AP(t, kl * 128,
                                              [[INTER, 128], [128, kh - kl],
                                               [1, 128]]),
                                      bass.AP(wd_d, kl * 128 * D + do * 128,
                                              [[D, 128], [128 * D, kh - kl],
                                               [1, 128]])),
                                  reads=[wchunk_bufs[("wd_d", k)]
                                         for k in range(klo, khi)],
                                  writes=[b_wdt[dpar]], dma=True)
                        for sub in range(2):
                            i = un % 4
                            un += 1
                            cc = c0 + sub * 512

                            def dmm(e, dpar=dpar, sub=sub, i=i):
                                last = None
                                for k in range(KI):
                                    last = e.matmul(
                                        psd[i][:, :],
                                        wdt[dpar][:, k * 128:(k + 1) * 128],
                                        hps[sub][:, k * 512:(k + 1) * 512],
                                        start=(k == 0), stop=(k == KI - 1))
                                return last
                            p.add("tensor", dmm,
                                  reads=[b_wdt[dpar], b_hps[sub]],
                                  writes=[b_psd[i]])
                            p.add("sync", lambda e, do=do, cc=cc, i=i:
                                  e.dma_start(
                                      rt[i][:, :],
                                      resid_d[do * 128:(do + 1) * 128,
                                              cc:cc + 512]),
                                  reads=[b_resid], writes=[b_rt[i]], dma=True)
                            p.add("vector", lambda e, i=i: e.tensor_tensor(
                                rn[i][:, :], psd[i][:, :], rt[i][:, :], ALU.add),
                                reads=[b_psd[i], b_rt[i]], writes=[b_rn[i]])
                            p.add("gpsimd", lambda e, do=do, cc=cc, i=i:
                                  e.dma_start(
                                      out_d[do * 128:(do + 1) * 128,
                                            cc:cc + 512], rn[i][:, :]),
                                  reads=[b_rn[i]], writes=[b_out], dma=True,
                                  store=True)
            p.barrier()

        def mha_proj(resid_d, b_resid):
            b_q2 = Buf("q2_dram")
            b_k2 = Buf("k2_dram")
            b_v2 = Buf("v2_dram")
            with contextlib.ExitStack() as ph:
                xnb = [sball(ph, "m_xn", [128, 2048], BF16) for _ in range(NK)]
                b_xnb = [Buf(f"m_xn{k}") for k in range(NK)]
                T = ln_tmps(ph, "mln")
                wsb = [sball(ph, "m_w", [128, D], BF16) for _ in range(NK)]
                b_wsb = [Buf(f"m_w{k}") for k in range(NK)]
                ps = [psall(ph, "m_ps", [128, 512]) for _ in range(4)]
                b_ps = [Buf(f"m_ps{i}") for i in range(4)]
                ev = [sball(ph, "m_ev", [128, 512], BF16) for _ in range(4)]
                b_ev = [Buf(f"m_ev{i}") for i in range(4)]

                for tb in range(2):
                    c0 = tb * 2048
                    for tg in range(4):
                        layernorm_tg(resid_d, b_resid, c0 + tg * 512,
                                     xnb, b_xnb, tg * 512, T)
                    for w_dram, dst_d, b_dst, tokmajor in (
                            (wq, q2_d, b_q2, False), (wk, k2_d, b_k2, False),
                            (wv, v_d, b_v2, True)):
                        for k in range(NK):
                            p.add("gpsimd", lambda e, k=k, w=w_dram:
                                  e.dma_start(wsb[k][:, :],
                                              w[k * 128:(k + 1) * 128, :]),
                                  writes=[b_wsb[k]], dma=True)
                        un = 0
                        if not tokmajor:
                            for do in range(NK):
                                for sub in range(4):
                                    i = un % 4
                                    un += 1

                                    def mmg(e, do=do, sub=sub, i=i):
                                        last = None
                                        for k in range(NK):
                                            last = e.matmul(
                                                ps[i][:, :],
                                                wsb[k][:, do * 128:(do + 1) * 128],
                                                xnb[k][:,
                                                       sub * 512:(sub + 1) * 512],
                                                start=(k == 0),
                                                stop=(k == NK - 1))
                                        return last
                                    p.add("tensor", mmg,
                                          reads=[b_wsb[k] for k in range(NK)] +
                                                b_xnb,
                                          writes=[b_ps[i]])
                                    p.add("scalar", lambda e, i=i: e.activation(
                                        ev[i][:, :], ps[i][:, :], AF.Copy),
                                        reads=[b_ps[i]], writes=[b_ev[i]])
                                    p.add("gpsimd", lambda e, do=do, c0=c0,
                                          sub=sub, i=i, d=dst_d, b=b_dst:
                                          e.dma_start(
                                              d[do * 128:(do + 1) * 128,
                                                c0 + sub * 512:
                                                c0 + (sub + 1) * 512],
                                              ev[i][:, :]),
                                          reads=[b_ev[i]], writes=[b_dst],
                                          dma=True, store=True)
                        else:
                            for tc in range(16):
                                for no in range(4):
                                    i = un % 4
                                    un += 1

                                    def mmg(e, tc=tc, no=no, i=i):
                                        last = None
                                        for k in range(NK):
                                            last = e.matmul(
                                                ps[i][:, :],
                                                xnb[k][:, tc * 128:(tc + 1) * 128],
                                                wsb[k][:, no * 512:(no + 1) * 512],
                                                start=(k == 0),
                                                stop=(k == NK - 1))
                                        return last
                                    p.add("tensor", mmg,
                                          reads=[b_wsb[k] for k in range(NK)] +
                                                b_xnb,
                                          writes=[b_ps[i]])
                                    p.add("scalar", lambda e, i=i: e.activation(
                                        ev[i][:, :], ps[i][:, :], AF.Copy),
                                        reads=[b_ps[i]], writes=[b_ev[i]])
                                    gtc = tb * 16 + tc
                                    p.add("gpsimd", lambda e, gtc=gtc, no=no,
                                          i=i: e.dma_start(
                                              v_d[gtc * 128:(gtc + 1) * 128,
                                                  no * 512:(no + 1) * 512],
                                              ev[i][:, :]),
                                          reads=[b_ev[i]], writes=[b_v2],
                                          dma=True, store=True)
            p.barrier()
            return b_q2, b_k2, b_v2

        b_ao = Buf("ao_dram")
        b_r1 = Buf("r1")
        b_r2 = Buf("r2")
        b_r3 = Buf("r3")
        b_outd = Buf("outd")

        if upto >= 5:
            attention(True, qc_d, b_qc, qr_d, b_qr, kc_d, b_kc, v_d, b_v,
                      ao_d, b_ao, sel=sel_jobs)
        if upto >= 6:
            out_proj(woT, ao_d, b_ao, x_feat, Buf("x2"), r1_d, b_r1)
        if upto >= 7:
            ffn(r1_d, b_r1, r2_d, b_r2, sel=sel_jobs_d)
        if upto >= 8:
            bq2, bk2, bv2 = mha_proj(r2_d, b_r2)
        if upto >= 9:
            attention(False, q2_d, bq2, None, None, k2_d, bk2, v_d, bv2,
                      ao_d, b_ao)
        if upto >= 10:
            out_proj(wo, ao_d, b_ao, r2_d, b_r2, r3_d, b_r3)
        if upto >= 11:
            ffn(r3_d, b_r3, out_feat, b_outd)
        if dbg:
            dbg_map = {"qc": qc_d, "qr": qr_d, "kc": kc_d, "v": v_d,
                       "ao": ao_d, "r1": r1_d, "r2": r2_d, "r3": r3_d,
                       "hp": hp_d, "wg": wg_d, "wu": wu_d, "wd": wd_d,
                       "q2": q2_d, "k2": k2_d}
            for name in dbg:
                srct = dbg_map[name]
                od = nc.dram_tensor(f"dbg_{name}", list(srct.shape),
                                    srct.dtype, kind="ExternalOutput")
                p.add("sync", lambda e, od=od, s=srct: e.dma_start(
                    od[:, :], s[:, :]), dma=True, store=True)
        p.barrier()

        N_DMA_SEMS = 90
        p.finalize(N_DMA_SEMS)
        csems = {e: es.enter_context(nc.semaphore(f"c_{e}")) for e in ENGS}
        dsems = [es.enter_context(nc.semaphore(f"dch_{i}"))
                 for i in range(N_DMA_SEMS)]
        block = es.enter_context(nc.Block())
        p.emit(block, csems, dsems)

    return nc


_NC_CACHE = {}


def kernel(**inputs):
    x = np.asarray(inputs["L0_attn_a_x"], dtype=np.float32)
    n = 8

    def t(a):
        return np.ascontiguousarray(np.asarray(a, dtype=np.float32).T)

    consts = _host_constants()
    shared = {
        "wdqT": t(inputs["L0_attn_a_W_DQ"]),
        "wuqT": t(inputs["L0_attn_a_W_UQ"]),
        "wqrT": t(inputs["L0_attn_a_W_QR"]),
        "wdkvT": t(inputs["L0_attn_a_W_DKV"]),
        "wukT": t(inputs["L0_attn_a_W_UK"]),
        "wuvT": t(inputs["L0_attn_a_W_UV"]),
        "wkrT": t(inputs["L0_attn_a_W_KR"]),
        "woT": t(inputs["L0_attn_a_W_O"]),
        "wq": np.ascontiguousarray(np.asarray(inputs["L0_attn_b_Wq"],
                                              dtype=np.float32)),
        "wk": np.ascontiguousarray(np.asarray(inputs["L0_attn_b_Wk"],
                                              dtype=np.float32)),
        "wv": np.ascontiguousarray(np.asarray(inputs["L0_attn_b_Wv"],
                                              dtype=np.float32)),
        "wo": np.ascontiguousarray(np.asarray(inputs["L0_attn_b_Wo"],
                                              dtype=np.float32)),
        "gidxT": np.ascontiguousarray(
            np.asarray(inputs["L0_ffn_gate_idx"], dtype=np.int32).T),
        "uidxT": np.ascontiguousarray(
            np.asarray(inputs["L0_ffn_up_idx"], dtype=np.int32).T),
        "didxT": np.ascontiguousarray(
            np.asarray(inputs["L0_ffn_down_idx"], dtype=np.int32).T),
        "lutrow": np.asarray(inputs["L0_ffn_lut"],
                             dtype=np.float32).reshape(1, 16),
        "cosq": consts["cosq"],
        "sinq": consts["sinq"],
        "perm": consts["perm"],
        "umask": consts["umask"],
        "ubar": consts["ubar"],
        "consts": consts["consts"],
    }

    in_maps = []
    for c in range(n):
        m = dict(shared)
        m["x_feat"] = np.ascontiguousarray(x[c].T)
        in_maps.append(m)

    if "nc" not in _NC_CACHE:
        _NC_CACHE["nc"] = build_nc()
    nc = _NC_CACHE["nc"]

    res = run_bass_kernel_spmd(nc, in_maps, core_ids=list(range(n)))
    out = np.empty((n, S, D), dtype=np.float32)
    for c in range(n):
        out[c] = res.results[c]["out_feat"].T
    return out
